# version 01
# baseline (speedup 1.0000x reference)
import contextlib
import numpy as np

import concourse.bass as bass
import concourse.mybir as mybir
from concourse.bass_utils import run_bass_kernel_spmd

F32 = mybir.dt.float32
BF16 = mybir.dt.bfloat16
I32 = mybir.dt.int32
ALU = mybir.AluOpType
ACT = mybir.ActivationFunctionType
AXL = mybir.AxisListType

C32 = float(1.5 * 2 ** 23)
C64 = float(1.5 * 2 ** 52)
T = 2048
D = 1024
NT = 16
EPS32 = float(2.0 ** -23)
EPS64 = float(2.0 ** -52)
FI = 4096
I1 = 2816
NIC = 6
IC_OFF = [0, 512, 1024, 1536, 2048, 2560]
IC_W = [512, 512, 512, 512, 512, 256]


def _pi_chunks(n=5, bits=11):
    import math
    rem = np.float64(math.pi)
    out = []
    for _ in range(n):
        if rem == 0:
            out.append(0.0)
            continue
        e = int(np.floor(np.log2(abs(rem))))
        sc = 2.0 ** (e - bits + 1)
        c = np.float64(np.round(rem / sc) * sc)
        out.append(float(np.float32(c)))
        rem = np.float64(rem) - np.float64(np.float32(c))
    return out


PI_CH = _pi_chunks()
PI_F32 = float(np.float32(np.pi))
PI_HALF = float(np.float32(np.pi / 2))
INV_PI = float(np.float32(1.0 / np.pi))
MIDS = [0.25, 0.75, 1.25, 1.75, 2.5, 3.5, 5.0]
BF16_1EM8 = 9.958720207214355e-09

ENGS = ("sync", "tensor", "vector", "scalar", "gpsimd")


class Chain:
    def __init__(self, nc, es):
        self.nc = nc
        self.sems = {e: es.enter_context(nc.semaphore("c_" + e)) for e in ENGS}
        self.cnt = {e: 0 for e in ENGS}
        self.steps = []
        self.last = None

    def step(self, eng, fn, dmas=0):
        prev = self.last
        inc = 16 * dmas if dmas else 1
        self.cnt[eng] += inc
        self.steps.append((eng, fn, prev, dmas))
        self.last = (eng, self.cnt[eng])

    def emit(self, fence_proxy=None):
        per_eng = {e: [] for e in ENGS}
        for eng, fn, prev, dmas in self.steps:
            per_eng[eng].append((fn, prev, dmas))
        sems = self.sems
        with self.nc.Block() as blk:
            def mk(eng):
                def body(h):
                    hw = h
                    if fence_proxy is not None and eng in ("vector", "scalar"):
                        hw = fence_proxy(h)
                    real_dma = h.dma_start
                    cnt = [0]

                    def counting_dma(*a, **k):
                        cnt[0] += 1
                        return real_dma(*a, **k)
                    for fn, prev, dmas in per_eng[eng]:
                        if prev is not None:
                            h.wait_ge(sems[prev[0]], prev[1])
                        if dmas:
                            cnt[0] = 0
                            h.dma_start = counting_dma
                            fn(h)
                            h.dma_start = real_dma
                            assert cnt[0] == dmas, (
                                f"step on {eng} declared {dmas} dmas, emitted {cnt[0]}: {fn}")
                        else:
                            ins = fn(hw)
                            if hw is not h:
                                h.sem_inc(sems[eng], 1)
                            else:
                                ins.then_inc(sems[eng], 1)
                return body
            for e in ENGS:
                getattr(blk, e)(mk(e))


def build_nc(debug_taps=(), upto=99.0):
    taps = set(debug_taps)
    nc = bass.Bass("TRN2", target_bir_lowering=False)

    x_in = nc.dram_tensor("x", [T, D], F32, kind="ExternalInput").ap()
    y_out = nc.dram_tensor("y", [T, D], F32, kind="ExternalOutput").ap()
    w_qkv = nc.dram_tensor("w_qkv", [1536, D], F32, kind="ExternalInput").ap()
    w_proj = nc.dram_tensor("w_proj", [D, D], F32, kind="ExternalInput").ap()
    q_gain = nc.dram_tensor("q_gain", [1, 16], F32, kind="ExternalInput").ap()
    w1T_d = nc.dram_tensor("w1T", [D, FI], F32, kind="ExternalInput").ap()
    convw_d = nc.dram_tensor("conv_w", [FI, 3], F32, kind="ExternalInput").ap()
    convb_d = nc.dram_tensor("conv_b", [FI, 1], F32, kind="ExternalInput").ap()
    w2T_d = nc.dram_tensor("w2T", [FI, D], F32, kind="ExternalInput").ap()
    wq1T_d = nc.dram_tensor("wq1T", [D, D], F32, kind="ExternalInput").ap()
    wk1T_d = nc.dram_tensor("wk1T", [D, D], F32, kind="ExternalInput").ap()
    wv1T_d = nc.dram_tensor("wv1T", [D, D], F32, kind="ExternalInput").ap()
    wo1T_d = nc.dram_tensor("wo1T", [D, D], F32, kind="ExternalInput").ap()
    wgate_d = nc.dram_tensor("w_gate", [I1, D], F32, kind="ExternalInput").ap()
    wup_d = nc.dram_tensor("w_up", [I1, D], F32, kind="ExternalInput").ap()
    wdown_d = nc.dram_tensor("w_down", [D, I1], F32, kind="ExternalInput").ap()
    cos0_d = nc.dram_tensor("cos0", [T, 32], F32, kind="ExternalInput").ap()
    sin0_d = nc.dram_tensor("sin0", [T, 32], F32, kind="ExternalInput").ap()
    ang1_d = nc.dram_tensor("ang1", [1, 32], F32, kind="ExternalInput").ap()
    qT_park = nc.dram_tensor("qT_park", [D, T], F32, kind="Internal").ap()
    kT_park = nc.dram_tensor("kT_park", [D, T], F32, kind="Internal").ap()
    v_park = nc.dram_tensor("v_park", [T, 16 * 65], F32, kind="Internal").ap()
    xnT_park = nc.dram_tensor("xnT_park", [D, T], F32, kind="Internal").ap()
    qxT_park = nc.dram_tensor("qxT_park", [D, T], BF16, kind="Internal").ap()
    xqT_park = nc.dram_tensor("xqT_park", [D, T], F32, kind="Internal").ap()
    h_park = nc.dram_tensor("h_park", [FI, T], F32, kind="Internal").ap()
    row_bnc = nc.dram_tensor("row_bnc", [128, 16], F32, kind="Internal").ap()

    dbg = {}

    def tap(name, shape, dt=F32):
        if name in taps:
            dbg[name] = nc.dram_tensor("dbg_" + name, shape, dt, kind="ExternalOutput").ap()
            return dbg[name]
        return None

    es = contextlib.ExitStack()
    with es:
        x_res = es.enter_context(nc.sbuf_tensor("x_res", [128, NT * D], F32))
        P1 = es.enter_context(nc.sbuf_tensor("P1", [128, 9216], F32))
        P2 = es.enter_context(nc.sbuf_tensor("P2", [128, 4096], F32))
        WP = es.enter_context(nc.sbuf_tensor("WP", [128, 12288], F32))
        WB = es.enter_context(nc.sbuf_tensor("WB", [128, 8192], BF16))
        cos0 = es.enter_context(nc.sbuf_tensor("cos0s", [128, 512], F32))
        sin0 = es.enter_context(nc.sbuf_tensor("sin0s", [128, 512], F32))
        cos1 = es.enter_context(nc.sbuf_tensor("cos1s", [128, 512], F32))
        sin1 = es.enter_context(nc.sbuf_tensor("sin1s", [128, 512], F32))
        ones_row = es.enter_context(nc.sbuf_tensor("ones_row", [1, 128], F32))
        ones_col = es.enter_context(nc.sbuf_tensor("ones_col", [128, 4], F32))
        ident = es.enter_context(nc.sbuf_tensor("ident", [128, 128], F32))
        tril = es.enter_context(nc.sbuf_tensor("tril", [128, 128], BF16))
        flag = es.enter_context(nc.sbuf_tensor("flag", [128, 8], F32))
        epsc = es.enter_context(nc.sbuf_tensor("epsc", [128, 24], F32))
        gain16 = es.enter_context(nc.sbuf_tensor("gain16", [128, 16], F32))
        st16 = es.enter_context(nc.sbuf_tensor("st16", [128, 256], F32))
        st8 = es.enter_context(nc.sbuf_tensor("st8", [128, 40], F32))
        rowA = es.enter_context(nc.sbuf_tensor("rowA", [1, T], F32))
        rowB = es.enter_context(nc.sbuf_tensor("rowB", [1, 256], F32))
        rowC = es.enter_context(nc.sbuf_tensor("rowC", [1, T], F32))
        cpool = es.enter_context(nc.sbuf_tensor("cpool", [128, 8], F32))
        pS = es.enter_context(nc.psum_tensor("pS", [128, 2048], F32))
        pA = es.enter_context(nc.psum_tensor("pA", [128, 1024], F32))
        pB = es.enter_context(nc.psum_tensor("pB", [128, 512], F32))
        pC = es.enter_context(nc.psum_tensor("pC", [128, 512], F32))

        CV = {val: i for i, val in enumerate((PI_HALF, -1.0, 1e-5, 6.0, 10000.0, 0.125, 2.0, 127.0))}

        def cvec(val):
            return cpool[:, CV[val]:CV[val] + 1]

        ch = Chain(nc, es)
        SY = ch.sems["sync"]
        auxv = es.enter_context(nc.semaphore("auxv"))
        auxs = es.enter_context(nc.semaphore("auxs"))
        AUXN = [0]
        AUXSN = [0]
        _NOFENCE = {"wait_ge", "sem_inc", "dma_start", "then_inc"}

        class FenceProxy:

            def __init__(self, h):
                self._h = h
                self._sem = auxv if h is not None else None

            def __getattr__(self, name):
                attr = getattr(self._h, name)
                if not callable(attr) or name.startswith("_") or name in _NOFENCE:
                    return attr
                def wrapped(*a, **k):
                    ins = attr(*a, **k)
                    try:
                        from concourse.bass_core_rust import BassInstruction as _BI
                        is_ins = isinstance(ins, _BI)
                    except Exception:
                        is_ins = hasattr(ins, "then_inc")
                    if is_ins:
                        import concourse.mybir as _mb
                        if getattr(self._h, "engine", None) == _mb.EngineType.Activation:
                            AUXSN[0] += 1
                            ins.then_inc(auxs, 1)
                            self._h.wait_ge(auxs, AUXSN[0])
                        else:
                            AUXN[0] += 1
                            ins.then_inc(auxv, 1)
                            self._h.wait_ge(auxv, AUXN[0])
                    return ins
                return wrapped

        def vf(v, ins):
            return ins

        def fbc(p, f):
            return flag[0:p, 0:1].broadcast_to((p, f))

        def rne(v, dst, src, tmpA, tmpB):
            return v.tensor_scalar(dst, src, C32, -C32, ALU.add, ALU.add)

        def mulb(v, out2d, in2d, col, b, i):
            return v.tensor_mul(out2d.rearrange("p (b i) -> p b i", i=i),
                                in2d.rearrange("p (b i) -> p b i", i=i),
                                col.unsqueeze(2).broadcast_to((128, b, i)))

        def fp4q(v, dst, src, ta, tb, tc, scA, scB, nblk):
            s3 = src.rearrange("p (b i) -> p b i", i=16)
            ins = v.reduce_max(scA.rearrange("p (g o) -> p g o", o=1), s3, axis=AXL.X, apply_absolute_value=True)
            vf(v, ins)
            ins = v.reciprocal(st8[:, 0:8], cpool[:, CV[6.0]:CV[6.0] + 1].broadcast_to((128, 8)))
            vf(v, ins)
            vf(v, v.tensor_mul(scB, scA, st8[:, 0:1].broadcast_to((128, nblk))))
            vf(v, v.tensor_scalar(scA, scB, 1e-12, None, ALU.max))
            vf(v, v.reciprocal(scB, scA))
            v.tensor_scalar(tc, src, -1.0, None, ALU.mult)
            v.tensor_max(ta, src, tc)
            mulb(v, tb, ta, scB, nblk, 16)
            v.memset(ta, 0.0)
            for m in MIDS:
                v.tensor_single_scalar(tc, tb, float(m), ALU.is_gt)
                v.tensor_copy(dst, ta)
                v.tensor_add(ta, dst, tc)
            v.tensor_scalar(tc, ta, -4.0, 0.0, ALU.add, ALU.max)
            v.tensor_scalar(dst, ta, 0.5, None, ALU.mult)
            v.tensor_scalar(tb, tc, 0.5, None, ALU.mult)
            v.tensor_add(tc, dst, tb)
            v.tensor_scalar(dst, ta, -6.0, 0.0, ALU.add, ALU.max)
            v.tensor_add(tb, tc, dst)
            v.tensor_single_scalar(ta, src, 0.0, ALU.is_lt)
            v.tensor_scalar(tc, ta, -2.0, 1.0, ALU.mult, ALU.add)
            v.tensor_mul(ta, tb, tc)
            return mulb(v, dst, ta, scA, nblk, 16)

        def resadd(v, qg, pAsrc, ncols, tmp):
            v.tensor_add(tmp, x_res[:, qg * D:qg * D + ncols], pAsrc)
            return v.tensor_copy(x_res[:, qg * D:qg * D + ncols], tmp)

        def _ld0(sy):
            sy.dma_start(out=x_res[:].rearrange("p (t d) -> p t d", d=D),
                         in_=x_in.rearrange("(t p) d -> p t d", p=128)).then_inc(SY, 16)
            sy.dma_start(out=cos0[:].rearrange("p (t c) -> p t c", c=32),
                         in_=cos0_d.rearrange("(t p) c -> p t c", p=128)).then_inc(SY, 16)
            sy.dma_start(out=sin0[:].rearrange("p (t c) -> p t c", c=32),
                         in_=sin0_d.rearrange("(t p) c -> p t c", p=128)).then_inc(SY, 16)
            sy.dma_start(out=rowA[0:1, 0:16], in_=q_gain).then_inc(SY, 16)
            sy.dma_start(out=rowA[0:1, 16:48], in_=ang1_d).then_inc(SY, 16)
        ch.step("sync", _ld0, dmas=5)

        def _iota0(g):
            g.iota(ident[:], [[1, 128]], base=0, channel_multiplier=-1,
                   allow_small_or_imprecise_dtypes=True)
            g.iota(st16[:, 128:256], [[1, 128]], base=0, channel_multiplier=-1,
                   allow_small_or_imprecise_dtypes=True)
            g.iota(st16[:, 0:16], [[128, 16]], base=0, channel_multiplier=1,
                   allow_small_or_imprecise_dtypes=True)
            return g.iota(rowB[0:1, 0:32], [[1, 32]], base=0, channel_multiplier=0,
                          allow_small_or_imprecise_dtypes=True)
        ch.step("gpsimd", _iota0)

        def _const0(v):
            v.tensor_single_scalar(ident[:], ident[:], 0.0, ALU.is_equal)
            v.tensor_single_scalar(tril[:], st16[:, 128:256], 0.0, ALU.is_le)
            v.memset(ones_row[:], 1.0)
            v.memset(ones_col[:], 1.0)
            for _val, _i in CV.items():
                v.memset(cpool[:, _i:_i + 1], _val)
            vf(v, v.tensor_scalar(flag[:], x_res[:, 0:8], 0.0, float(2.0 ** -40), ALU.mult, ALU.add))
            vf(v, v.tensor_scalar(st8[:, 0:8], flag[:], 1.0, -1.0, ALU.add, ALU.add))
            vf(v, v.tensor_scalar(flag[:], st8[:, 0:8], float(2.0 ** 40), None, ALU.mult))
            ins = v.tensor_scalar(st8[:, 0:8], flag[:], EPS64 - EPS32, None, ALU.mult)
            vf(v, ins)
            return v.tensor_single_scalar(epsc[:], st8[:, 0:1].broadcast_to((128, 24)), EPS32, ALU.add)
        ch.step("vector", _const0)

        ch.step("tensor", lambda t: t.matmul(pB[:, 0:16], ones_row[:], rowA[0:1, 0:16],
                                             start=True, stop=True))
        ch.step("vector", lambda v: v.tensor_scalar(gain16[:], pB[:, 0:16], 0.125, None, ALU.mult))

        TH_, KK_, K1_, K0_ = P1[:, 0:512], P1[:, 512:1024], P1[:, 1024:1536], P1[:, 1536:2048]
        RR_, SG_, T1_, T2_ = P1[:, 2048:2560], P1[:, 2560:3072], P1[:, 3072:3584], P1[:, 3584:4096]
        T3_, ANB = P1[:, 4096:4608], P1[:, 4608:5120]

        ch.step("vector", lambda v: v.tensor_scalar(rowB[0:1, 32:64], rowB[0:1, 0:32],
                                                    float(1.0 / 32.0), None, ALU.mult))

        ch.step("vector", lambda v: v.memset(rowC[0:1, 0:8], 10000.0))

        def _lnexp(s):
            s.activation(rowC[0:1, 8:16], rowC[0:1, 0:8], ACT.Ln)
            return s.activation(rowB[0:1, 64:96], rowB[0:1, 32:64], ACT.Exp, scale=rowC[0:1, 8:9])
        ch.step("scalar", _lnexp)

        def _angf(v):
            vf(v, v.tensor_sub(rowB[0:1, 96:128], rowB[0:1, 64:96], rowA[0:1, 16:48]))
            vf(v, v.tensor_mul(rowC[0:1, 8:40], rowB[0:1, 96:128], flag[0:1, 0:1].broadcast_to((1, 32))))
            return v.tensor_add(rowB[0:1, 128:160], rowC[0:1, 8:40], rowA[0:1, 16:48])
        ch.step("vector", _angf)

        ch.step("tensor", lambda t: t.matmul(pB[:, 0:32], ones_row[:], rowB[0:1, 128:160],
                                             start=True, stop=True))

        def _tbl(v):
            v.tensor_copy(ANB.rearrange("p (t i) -> p t i", i=32),
                          pB[:, 0:32].unsqueeze(1).broadcast_to((128, 16, 32)))
            v.tensor_mul(TH_.rearrange("p (t i) -> p t i", i=32),
                         ANB.rearrange("p (t i) -> p t i", i=32),
                         st16[:, 0:16].unsqueeze(2).broadcast_to((128, 16, 32)))
            v.tensor_scalar(T3_, TH_, INV_PI, None, ALU.mult)
            rne(v, KK_, T3_, T1_, T2_)
            v.tensor_scalar(T3_, KK_, float(2.0 ** -11), None, ALU.mult)
            rne(v, K1_, T3_, T1_, T2_)
            v.scalar_tensor_tensor(K0_, K1_, -2048.0, KK_, ALU.mult, ALU.add)
            v.tensor_copy(RR_, TH_)
            for c in PI_CH:
                v.scalar_tensor_tensor(T1_, K1_, -2048.0 * c, RR_, ALU.mult, ALU.add)
                v.scalar_tensor_tensor(RR_, K0_, -c, T1_, ALU.mult, ALU.add)
            v.tensor_scalar(T3_, KK_, 0.5, None, ALU.mult)
            rne(v, T2_, T3_, T1_, ANB)
            v.scalar_tensor_tensor(SG_, T2_, -2.0, KK_, ALU.mult, ALU.add)
            v.tensor_mul(T1_, SG_, SG_)
            v.tensor_scalar(SG_, T1_, -2.0, 1.0, ALU.mult, ALU.add)
            for _ in range(2):
                v.tensor_single_scalar(T1_, RR_, PI_HALF, ALU.is_gt)
                v.scalar_tensor_tensor(T2_, T1_, -PI_F32, RR_, ALU.mult, ALU.add)
                v.tensor_copy(RR_, T2_)
                v.tensor_scalar(T2_, T1_, -2.0, 1.0, ALU.mult, ALU.add)
                v.tensor_mul(T3_, SG_, T2_)
                v.tensor_copy(SG_, T3_)
                v.tensor_single_scalar(T1_, RR_, -PI_HALF, ALU.is_lt)
                v.scalar_tensor_tensor(T2_, T1_, PI_F32, RR_, ALU.mult, ALU.add)
                v.tensor_copy(RR_, T2_)
                v.tensor_scalar(T2_, T1_, -2.0, 1.0, ALU.mult, ALU.add)
                v.tensor_mul(T3_, SG_, T2_)
                v.tensor_copy(SG_, T3_)
            return v.tensor_copy(T3_, RR_)
        ch.step("vector", _tbl)

        def _sins(s):
            s.activation(T1_, RR_, ACT.Sin)
            s.activation(T2_, RR_, ACT.Abs)
            return s.activation(T3_, T2_, ACT.Sin, bias=cvec(PI_HALF), scale=cvec(-1.0))
        ch.step("scalar", _sins)

        def _tblf(v):
            v.tensor_mul(sin1[:], T1_, SG_)
            return v.tensor_mul(cos1[:], T3_, SG_)
        ch.step("vector", _tblf)

        if tap("cos1", [128, 512]) is not None:
            tap("sin1", [128, 512])

            def _tapc(sy):
                sy.dma_start(out=dbg["cos1"], in_=cos1[:]).then_inc(SY, 16)
                sy.dma_start(out=dbg["sin1"], in_=sin1[:]).then_inc(SY, 16)
            ch.step("sync", _tapc, dmas=2)

        def emit_ln_prep(li):
            MEANS = st16[:, 16:32]
            RSTD = st16[:, 32:48]

            def _ln1(v):
                dv = P1[:, 0:1024]
                sq = P1[:, 1024:2048]
                su = st16[:, 48:64]
                ssq = st16[:, 64:80]
                for ti in range(NT):
                    ins = v.reduce_sum(su[:, ti:ti + 1], x_res[:, ti * D:(ti + 1) * D], axis=AXL.X)
                vf(v, ins)
                ins = v.tensor_scalar(MEANS, su, float(1.0 / D), None, ALU.mult)
                vf(v, ins)
                for ti in range(NT):
                    v.tensor_sub(dv, x_res[:, ti * D:(ti + 1) * D],
                                 MEANS[:, ti:ti + 1].broadcast_to((128, D)))
                    v.tensor_mul(sq, dv, dv)
                    ins = v.reduce_sum(ssq[:, ti:ti + 1], sq, axis=AXL.X)
                vf(v, ins)
                return v.tensor_scalar(st8[:, 0:16], ssq, float(1.0 / D), 1e-5, ALU.mult, ALU.add)
            ch.step("vector", _ln1)
            ch.step("scalar", lambda s: s.activation(st8[:, 16:32], st8[:, 0:16], ACT.Sqrt))
            ch.step("vector", lambda v: v.reciprocal(RSTD, st8[:, 16:32]))

            xn1 = P1[:, 0:1024]
            qx1 = P1[:, 1024:2048]
            ta1 = P1[:, 2048:3072]
            tb1 = P1[:, 3072:4096]
            tc1 = P1[:, 4096:5120]
            xnT1 = P1[:, 5120:6144]
            qxT1 = P1[:, 6144:7168]
            for ti in range(NT):
                def _xt(v, ti=ti):
                    v.tensor_sub(ta1, x_res[:, ti * D:(ti + 1) * D],
                                 MEANS[:, ti:ti + 1].broadcast_to((128, D)))
                    v.tensor_mul(xn1, ta1, RSTD[:, ti:ti + 1].broadcast_to((128, D)))
                    if li == 0:
                        ins = v.reduce_max(st16[:, 80 + ti:81 + ti], xn1, axis=AXL.X, apply_absolute_value=True)
                        vf(v, ins)
                        vf(v, v.tensor_scalar(st8[:, 8:16], st16[:, 73 + ti:81 + ti], 1e-5, None, ALU.max))
                        vf(v, v.reciprocal(st8[:, 16:24], st8[:, 8:16]))
                        vf(v, v.tensor_scalar(st8[:, 24:32], st8[:, 16:24], 127.0, None, ALU.mult))
                        v.tensor_mul(ta1, xn1, st8[:, 31:32].broadcast_to((128, D)))
                        return rne(v, qx1, ta1, tb1, tc1)
                    else:
                        return fp4q(v, qx1, xn1, ta1, tb1, tc1,
                                    st16[:, 112:176], st16[:, 176:240], nblk=64)
                ch.step("vector", _xt)

                def _xtr(t, ti=ti):
                    for i in range(8):
                        t.transpose(pS[:, i * 128:(i + 1) * 128], xn1[:, i * 128:(i + 1) * 128], ident[:])
                    for i in range(8):
                        ins = t.transpose(pS[:, 1024 + i * 128:1152 + i * 128],
                                          qx1[:, i * 128:(i + 1) * 128], ident[:])
                    return ins
                ch.step("tensor", _xtr)

                def _xev(v, ti=ti):
                    v.tensor_copy(xnT1, pS[:, 0:1024])
                    if li == 0:
                        return v.tensor_copy(WB[:, 0:1024], pS[:, 1024:2048])
                    return v.tensor_copy(qxT1, pS[:, 1024:2048])
                ch.step("vector", _xev)

                def _xpk(sy, ti=ti):
                    sy.dma_start(out=xnT_park.rearrange("(a p) t -> p a t", p=128)[:, :, ti * 128:(ti + 1) * 128],
                                 in_=xnT1.rearrange("p (a c) -> p a c", c=128)).then_inc(SY, 16)
                    if li == 0:
                        sy.dma_start(out=qxT_park.rearrange("(a p) t -> p a t", p=128)[:, :, ti * 128:(ti + 1) * 128],
                                     in_=WB[:, 0:1024].rearrange("p (a c) -> p a c", c=128)).then_inc(SY, 16)
                    else:
                        sy.dma_start(out=xqT_park.rearrange("(a p) t -> p a t", p=128)[:, :, ti * 128:(ti + 1) * 128],
                                     in_=qxT1.rearrange("p (a c) -> p a c", c=128)).then_inc(SY, 16)
                ch.step("sync", _xpk, dmas=2)

        def emit_l0_actscales():
            def _sx(v):
                vf(v, v.tensor_scalar(st8[:, 0:16], st16[:, 80:96], 1e-5, None, ALU.max))
                vf(v, v.reciprocal(st8[:, 16:32], st8[:, 0:16]))
                v.tensor_scalar(st16[:, 96:112], st8[:, 0:16], float(1.0 / 127.0), None, ALU.mult)
                return v.tensor_scalar(st16[:, 80:96], st8[:, 16:32], 127.0, None, ALU.mult)
            ch.step("vector", _sx)

        def emit_ternary_wqkv():
            wn = P1[:, 0:1024]
            wf = P1[:, 1024:2048]
            wq_ = P1[:, 2048:3072]
            ta_ = P1[:, 3072:4096]
            tb_ = P1[:, 4096:5120]
            wb16 = WB[:, 0:1024]
            gs16 = WB[:, 1024:1040]
            for ci in range(12):
                def _wld(sy, ci=ci):
                    sy.dma_start(out=wn, in_=w_qkv[ci * 128:(ci + 1) * 128, :]).then_inc(SY, 16)
                ch.step("sync", _wld, dmas=1)

                def _wq(v, ci=ci):
                    gsA = st16[:, 240:256]
                    gsB = st16[:, 224:240]
                    v.tensor_copy(wb16, wn)
                    v.tensor_copy(wf, wb16)
                    v.tensor_scalar(wq_, wf, -1.0, None, ALU.mult)
                    v.tensor_max(ta_, wf, wq_)
                    ins = v.reduce_sum(gsA.rearrange("p (g o) -> p g o", o=1), ta_.rearrange("p (g i) -> p g i", i=64), axis=AXL.X)
                    vf(v, ins)
                    vf(v, v.tensor_scalar(gsB, gsA, float(1.0 / 64.0), None, ALU.mult))
                    v.tensor_scalar(gs16, gsB, 1.0, None, ALU.mult)
                    v.tensor_scalar(gsA, gs16, 1.0, None, ALU.mult)
                    vf(v, v.tensor_scalar(gsB, gsA, BF16_1EM8, None, ALU.max))
                    vf(v, v.reciprocal(gsA, gsB))
                    mulb(v, ta_, wf, gsA, 16, 64)
                    v.tensor_copy(wb16, ta_)
                    v.tensor_copy(ta_, wb16)
                    rne(v, wq_, ta_, tb_, wn)
                    ins = v.tensor_scalar(ta_, wq_, 1.0, -1.0, ALU.min, ALU.max)
                    vf(v, ins)
                    mulb(v, wq_, ta_, gsB, 16, 64)
                    return v.tensor_copy(wf.bitcast(I32), wq_.bitcast(I32))
                ch.step("vector", _wq)

                def _wtr(t, ci=ci):
                    for i in range(8):
                        ins = t.transpose(pS[:, i * 128:(i + 1) * 128], wf[:, i * 128:(i + 1) * 128], ident[:])
                    return ins
                ch.step("tensor", _wtr)

                def _wev(v, ci=ci):
                    return v.tensor_copy(WP[:].rearrange("p (a c) -> p a c", c=1536)[:, :, ci * 128:(ci + 1) * 128],
                                         pS[:, 0:1024].rearrange("p (a c) -> p a c", c=128))
                ch.step("vector", _wev)

        def emit_ternary_wproj():
            wn = P1[:, 0:1024]
            wf = P1[:, 1024:2048]
            wq_ = P1[:, 2048:3072]
            ta_ = P1[:, 3072:4096]
            tb_ = P1[:, 4096:5120]
            wb16 = WB[:, 0:1024]
            gs16 = WB[:, 1024:1040]
            for ci in range(8):
                def _wld(sy, ci=ci):
                    sy.dma_start(out=wn, in_=w_proj[ci * 128:(ci + 1) * 128, :]).then_inc(SY, 16)
                ch.step("sync", _wld, dmas=1)

                def _wq(v, ci=ci):
                    gsA = st16[:, 240:256]
                    gsB = st16[:, 224:240]
                    v.tensor_copy(wb16, wn)
                    v.tensor_copy(wf, wb16)
                    v.tensor_scalar(wq_, wf, -1.0, None, ALU.mult)
                    v.tensor_max(ta_, wf, wq_)
                    ins = v.reduce_sum(gsA.rearrange("p (g o) -> p g o", o=1), ta_.rearrange("p (g i) -> p g i", i=64), axis=AXL.X)
                    vf(v, ins)
                    vf(v, v.tensor_scalar(gsB, gsA, float(1.0 / 64.0), None, ALU.mult))
                    v.tensor_scalar(gs16, gsB, 1.0, None, ALU.mult)
                    v.tensor_scalar(gsA, gs16, 1.0, None, ALU.mult)
                    vf(v, v.tensor_scalar(gsB, gsA, BF16_1EM8, None, ALU.max))
                    vf(v, v.reciprocal(gsA, gsB))
                    mulb(v, ta_, wf, gsA, 16, 64)
                    v.tensor_copy(wb16, ta_)
                    v.tensor_copy(ta_, wb16)
                    rne(v, wq_, ta_, tb_, wn)
                    ins = v.tensor_scalar(ta_, wq_, 1.0, -1.0, ALU.min, ALU.max)
                    vf(v, ins)
                    mulb(v, wq_, ta_, gsB, 16, 64)
                    return v.tensor_copy(wf.bitcast(I32), wq_.bitcast(I32))
                ch.step("vector", _wq)

                def _wtr(t, ci=ci):
                    for i in range(8):
                        ins = t.transpose(pS[:, i * 128:(i + 1) * 128], wf[:, i * 128:(i + 1) * 128], ident[:])
                    return ins
                ch.step("tensor", _wtr)

                def _wev(v, ci=ci):
                    return v.tensor_copy(WP[:, 0:8192].rearrange("p (a c) -> p a c", c=1024)[:, :, ci * 128:(ci + 1) * 128],
                                         pS[:, 0:1024].rearrange("p (a c) -> p a c", c=128))
                ch.step("vector", _wev)

        def emit_qkv_l0():
            xnTt = P1[:, 0:1024]
            qn_ = P1[:, 1024:2048]
            kn_ = P1[:, 2048:3072]
            qr_ = P1[:, 3072:4096]
            kr_ = P1[:, 4096:5120]
            sqA = P1[:, 5120:6144]
            sqB = P1[:, 6144:7168]
            vb_ = P1[:, 7168:8208]
            for ti in range(NT):
                def _qld(sy, ti=ti):
                    sy.dma_start(out=xnTt.rearrange("p (a c) -> p a c", c=128),
                                 in_=xnT_park.rearrange("(a p) t -> p a t", p=128)[:, :, ti * 128:(ti + 1) * 128]).then_inc(SY, 16)
                ch.step("sync", _qld, dmas=1)

                def _qmm(t, ti=ti):
                    for kd in range(8):
                        for n0 in range(0, 1536, 512):
                            ins = t.matmul(pS[:, n0:n0 + 512], xnTt[:, kd * 128:(kd + 1) * 128],
                                           WP[:, kd * 1536 + n0:kd * 1536 + n0 + 512],
                                           start=(kd == 0), stop=(kd == 7))
                    return ins
                ch.step("tensor", _qmm)

                def _qsq(s, ti=ti):
                    return s.activation(P1[:, 3072:4352], pS[:, 0:1280], ACT.Square)
                ch.step("scalar", _qsq)

                def _qrms(v, ti=ti):
                    v.reduce_sum(st16[:, 240:256].rearrange("p (g o) -> p g o", o=1),
                                 P1[:, 3072:4096].rearrange("p (g i) -> p g i", i=64), axis=AXL.X)
                    ins = v.reduce_sum(st16[:, 236:240].rearrange("p (g o) -> p g o", o=1),
                                 P1[:, 4096:4352].rearrange("p (g i) -> p g i", i=64), axis=AXL.X)
                    vf(v, ins)
                    vf(v, v.tensor_scalar(st16[:, 216:236], st16[:, 236:256], float(1.0 / 64.0), None, ALU.mult))
                    return v.tensor_add(st16[:, 196:216], st16[:, 216:236], epsc[:, 0:20])
                ch.step("vector", _qrms)
                ch.step("scalar", lambda s: s.activation(st16[:, 216:236], st16[:, 196:216], ACT.Sqrt))

                def _qnrm(v, ti=ti):
                    rs = st16[:, 196:216]
                    ins = v.reciprocal(rs, st16[:, 216:236])
                    vf(v, ins)
                    mulb(v, kn_[:, 0:256], pS[:, 1024:1280], rs[:, 0:4], 4, 64)
                    return mulb(v, qn_, pS[:, 0:1024], rs[:, 4:20], 16, 64)
                ch.step("vector", _qnrm)

                def _rope(v, ti=ti):
                    cs = cos0[:, ti * 32:(ti + 1) * 32]
                    sn = sin0[:, ti * 32:(ti + 1) * 32]
                    for (src, dst, nh) in ((qn_, qr_, 16), (kn_, kr_, 4)):
                        s4 = src.rearrange("p (h z c) -> p h z c", z=2, c=32)
                        d4 = dst.rearrange("p (h z c) -> p h z c", z=2, c=32)
                        csb = cs.unsqueeze(1).broadcast_to((128, nh, 32))
                        snb = sn.unsqueeze(1).broadcast_to((128, nh, 32))
                        a = sqA[:, 0:nh * 32].rearrange("p (h c) -> p h c", c=32)
                        b = sqB[:, 0:nh * 32].rearrange("p (h c) -> p h c", c=32)
                        v.tensor_mul(a, s4[:, 0:nh, 0, :], csb)
                        v.tensor_mul(b, s4[:, 0:nh, 1, :], snb)
                        v.tensor_add(d4[:, 0:nh, 0, :], a, b)
                        v.tensor_mul(a, s4[:, 0:nh, 1, :], csb)
                        v.tensor_mul(b, s4[:, 0:nh, 0, :], snb)
                        v.tensor_sub(d4[:, 0:nh, 1, :], a, b)
                    v.memset(vb_, 0.0)
                    v.tensor_copy(vb_.rearrange("p (h c) -> p h c", c=65)[:, 0:4, 0:64],
                                  pS[:, 1280:1536].rearrange("p (h c) -> p h c", c=64))
                    return v.memset(vb_.rearrange("p (h c) -> p h c", c=65)[:, 0:4, 64:65], 1.0)
                ch.step("vector", _rope)

                def _qtr(t, ti=ti):
                    for i in range(8):
                        t.transpose(pS[:, i * 128:(i + 1) * 128], qr_[:, i * 128:(i + 1) * 128], ident[:])
                    for i in range(2):
                        ins = t.transpose(pS[:, 1024 + i * 128:1152 + i * 128],
                                          kr_[:, i * 128:(i + 1) * 128], ident[:])
                    return ins
                ch.step("tensor", _qtr)

                def _qev(v, ti=ti):
                    v.tensor_copy(qr_, pS[:, 0:1024])
                    return v.tensor_copy(kr_[:, 0:256], pS[:, 1024:1280])
                ch.step("vector", _qev)

                def _qpk(sy, ti=ti):
                    sy.dma_start(out=qT_park.rearrange("(a p) t -> p a t", p=128)[:, :, ti * 128:(ti + 1) * 128],
                                 in_=qr_.rearrange("p (a c) -> p a c", c=128)).then_inc(SY, 16)
                    sy.dma_start(out=kT_park.rearrange("(a p) t -> p a t", p=128)[:, 0:2, ti * 128:(ti + 1) * 128],
                                 in_=kr_[:, 0:256].rearrange("p (a c) -> p a c", c=128)).then_inc(SY, 16)
                    sy.dma_start(out=v_park[ti * 128:(ti + 1) * 128, :], in_=vb_).then_inc(SY, 16)
                ch.step("sync", _qpk, dmas=3)

        def emit_qkv_l1():
            xnTt = P1[:, 0:1024]
            pr_ = P1[:, 1024:2048]
            rr2 = P1[:, 2048:3072]
            sqA = P1[:, 3072:4096]
            sqB = P1[:, 4096:5120]
            vb_ = P1[:, 7168:8208]
            for mi, wd in enumerate((wq1T_d, wk1T_d, wv1T_d)):
                def _wl(sy, wd=wd):
                    sy.dma_start(out=WP[:, 0:8192].rearrange("p (a c) -> p a c", c=1024),
                                 in_=wd.rearrange("(a p) c -> p a c", p=128)).then_inc(SY, 16)
                ch.step("sync", _wl, dmas=1)
                for ti in range(NT):
                    def _qld(sy, ti=ti):
                        sy.dma_start(out=xnTt.rearrange("p (a c) -> p a c", c=128),
                                     in_=xnT_park.rearrange("(a p) t -> p a t", p=128)[:, :, ti * 128:(ti + 1) * 128]).then_inc(SY, 16)
                    ch.step("sync", _qld, dmas=1)

                    def _mm(t, ti=ti):
                        for kd in range(8):
                            for n0 in (0, 512):
                                ins = t.matmul(pS[:, n0:n0 + 512], xnTt[:, kd * 128:(kd + 1) * 128],
                                               WP[:, kd * 1024 + n0:kd * 1024 + n0 + 512],
                                               start=(kd == 0), stop=(kd == 7))
                        return ins
                    ch.step("tensor", _mm)

                    if mi < 2:
                        def _rope1(v, ti=ti):
                            cs = cos1[:, ti * 32:(ti + 1) * 32]
                            sn = sin1[:, ti * 32:(ti + 1) * 32]
                            s4 = pS[:, 0:1024].rearrange("p (h c z) -> p h c z", c=32, z=2)
                            d4 = pr_.rearrange("p (h c z) -> p h c z", c=32, z=2)
                            csb = cs.unsqueeze(1).broadcast_to((128, 16, 32))
                            snb = sn.unsqueeze(1).broadcast_to((128, 16, 32))
                            a = sqA.rearrange("p (h c) -> p h c", c=64)[:, :, 0:32]
                            b = sqB.rearrange("p (h c) -> p h c", c=64)[:, :, 0:32]
                            v.tensor_mul(a, s4[:, :, :, 0], csb)
                            v.tensor_mul(b, s4[:, :, :, 1], snb)
                            v.tensor_sub(d4[:, :, :, 0], a, b)
                            v.tensor_mul(a, s4[:, :, :, 0], snb)
                            v.tensor_mul(b, s4[:, :, :, 1], csb)
                            return v.tensor_add(d4[:, :, :, 1], a, b)
                        ch.step("vector", _rope1)

                        def _tr1(t, ti=ti):
                            for i in range(8):
                                ins = t.transpose(pS[:, 1024 + i * 128:1152 + i * 128],
                                                  pr_[:, i * 128:(i + 1) * 128], ident[:])
                            return ins
                        ch.step("tensor", _tr1)
                        ch.step("vector", lambda v: v.tensor_copy(rr2, pS[:, 1024:2048]))

                        def _pk1(sy, ti=ti, mi=mi):
                            park = qT_park if mi == 0 else kT_park
                            sy.dma_start(out=park.rearrange("(a p) t -> p a t", p=128)[:, :, ti * 128:(ti + 1) * 128],
                                         in_=rr2.rearrange("p (a c) -> p a c", c=128)).then_inc(SY, 16)
                        ch.step("sync", _pk1, dmas=1)
                    else:
                        def _vaug(v, ti=ti):
                            v.memset(vb_, 0.0)
                            v.tensor_copy(vb_.rearrange("p (h c) -> p h c", c=65)[:, :, 0:64],
                                          pS[:, 0:1024].rearrange("p (h c) -> p h c", c=64))
                            return v.memset(vb_.rearrange("p (h c) -> p h c", c=65)[:, :, 64:65], 1.0)
                        ch.step("vector", _vaug)

                        def _pkv(sy, ti=ti):
                            sy.dma_start(out=v_park[ti * 128:(ti + 1) * 128, :], in_=vb_).then_inc(SY, 16)
                        ch.step("sync", _pkv, dmas=1)

        def emit_attn(li):
            qTg = P1[:, 0:256]
            kTh = P1[:, 256:2304]
            vbh = P1[:, 2304:3344]
            ebuf = P1[:, 3344:5392]
            ySb = P1[:, 5392:7440]
            ynb = P1[:, 3344:5392]
            sqb = P2[:, 0:2048]
            msk = st16[:, 128:256][:, 0:128]
            for g in range(8):
                gmax = (2 * g + 2) * 128 if li == 0 else T
                nkt = gmax // 128
                for h in range(16):
                    kvh = (h // 4) if li == 0 else h

                    def _ald(sy, g=g, h=h, kvh=kvh, gmax=gmax):
                        sy.dma_start(out=qTg[0:64, :],
                                     in_=qT_park[h * 64:(h + 1) * 64, g * 256:(g + 1) * 256]).then_inc(SY, 16)
                        sy.dma_start(out=kTh[0:64, 0:gmax],
                                     in_=kT_park[kvh * 64:kvh * 64 + 64, 0:gmax]).then_inc(SY, 16)
                        sy.dma_start(out=vbh[:, 0:(gmax // 128) * 65].rearrange("p (a c) -> p a c", c=65),
                                     in_=v_park.rearrange("(a p) c -> p a c", p=128)[:, 0:gmax // 128, kvh * 65:(kvh + 1) * 65]).then_inc(SY, 16)
                    ch.step("sync", _ald, dmas=3)

                    for qi in range(2):
                        qg = 2 * g + qi
                        ext = (qg + 1) * 128 if li == 0 else T

                        def _sc(t, qi=qi, ext=ext):
                            for n0 in range(0, ext, 512):
                                n1 = min(n0 + 512, ext)
                                ins = t.matmul(pS[:, n0:n1], qTg[0:64, qi * 128:(qi + 1) * 128],
                                               kTh[0:64, n0:n1], start=True, stop=True)
                            return ins
                        ch.step("tensor", _sc)

                        sclAP = gain16[:, h:h + 1] if li == 0 else cvec(0.125)

                        def _mx(v, h=h, ext=ext, sclAP=sclAP):
                            ins = v.reduce_max(st8[:, 0:1], pS[:, 0:ext], axis=AXL.X, negate=True)
                            vf(v, ins)
                            return v.tensor_mul(st8[:, 8:16], st8[:, 0:8],
                                                sclAP.broadcast_to((128, 8)))
                        ch.step("vector", _mx)

                        def _ex(s, h=h, ext=ext, sclAP=sclAP):
                            return s.activation(ebuf[:, 0:ext], pS[:, 0:ext], ACT.Exp,
                                                scale=sclAP, bias=st8[:, 8:9])
                        ch.step("scalar", _ex)

                        if li == 0:
                            def _msk(v, ext=ext):
                                v.tensor_mul(msk, ebuf[:, ext - 128:ext], tril[:])
                                return v.tensor_scalar(ebuf[:, ext - 128:ext], msk, 1.0, None, ALU.mult)
                            ch.step("vector", _msk)

                        def _etr(t, ext=ext):
                            for kt in range(ext // 128):
                                ins = t.transpose(pS[:, kt * 128:(kt + 1) * 128],
                                                  ebuf[:, kt * 128:(kt + 1) * 128], ident[:])
                            return ins
                        ch.step("tensor", _etr)

                        def _eev(v, qi=qi, ext=ext):
                            kte = ext // 128
                            return v.tensor_copy(
                                P2[:, 0:kte * 256].rearrange("p (k c) -> p k c", c=256)[:, :, qi * 128:(qi + 1) * 128],
                                pS[:, 0:ext].rearrange("p (k c) -> p k c", c=128))
                        ch.step("vector", _eev)

                    def _av(t, g=g, li=li, nkt=nkt):
                        first = True
                        for kt in range(nkt):
                            sq0 = max(0, kt - 2 * g) if li == 0 else 0
                            if sq0 >= 2:
                                continue
                            ins = t.matmul(pB[0:65, sq0 * 128:256],
                                           vbh[:, kt * 65:(kt + 1) * 65],
                                           P2[:, kt * 256 + sq0 * 128:kt * 256 + 256],
                                           start=first, stop=(kt == nkt - 1))
                            first = False
                        return ins
                    ch.step("tensor", _av)

                    ch.step("vector", lambda v: v.reciprocal(rowB[0:1, 0:256], pB[64:65, 0:256]))
                    ch.step("tensor", lambda t: t.matmul(pC[0:64, 0:256], ones_row[0:1, 0:64],
                                                         rowB[0:1, 0:256], start=True, stop=True))
                    ch.step("vector", lambda v: v.tensor_copy(P1[0:64, 7440:7696], pC[0:64, 0:256]))

                    def _ysc(v, h=h):
                        return v.tensor_mul(ySb[(h % 2) * 64:(h % 2) * 64 + 64, (h // 2) * 256:(h // 2) * 256 + 256],
                                            pB[0:64, 0:256], P1[0:64, 7440:7696])
                    ch.step("vector", _ysc)

                if li == 0:
                    ch.step("scalar", lambda s: s.activation(sqb, ySb, ACT.Square))

                    def _ss(t):
                        for dt_ in range(8):
                            ins = t.matmul(pC[0:1, 0:256], ones_col[:, 0:1], sqb[:, dt_ * 256:(dt_ + 1) * 256],
                                           start=(dt_ == 0), stop=(dt_ == 7))
                        return ins
                    ch.step("tensor", _ss)

                    def _rmrow(v):
                        vf(v, v.tensor_scalar(rowB[0:1, 0:256], pC[0:1, 0:256], float(1.0 / D), None, ALU.mult))
                        return v.tensor_add(rowC[0:1, 0:256], rowB[0:1, 0:256],
                                            epsc[0:1, 0:1].broadcast_to((1, 256)))
                    ch.step("vector", _rmrow)
                    ch.step("scalar", lambda s: s.activation(rowC[0:1, 256:512], rowC[0:1, 0:256], ACT.Sqrt))
                    ch.step("vector", lambda v: v.reciprocal(rowC[0:1, 512:768], rowC[0:1, 256:512]))
                    ch.step("tensor", lambda t: t.matmul(pC[:, 0:256], ones_row[:], rowC[0:1, 512:768],
                                                         start=True, stop=True))

                    def _ynm(v):
                        return v.tensor_mul(ynb.rearrange("p (a c) -> p a c", c=256),
                                            ySb.rearrange("p (a c) -> p a c", c=256),
                                            pC[:, 0:256].unsqueeze(1).broadcast_to((128, 8, 256)))
                    ch.step("vector", _ynm)
                    ysrc = ynb
                else:
                    ysrc = ySb

                def _proj(t, ysrc=ysrc):
                    for dmo in range(8):
                        for dmy in range(8):
                            ins = t.matmul(pS[:, dmo * 256:(dmo + 1) * 256],
                                           WP[:, dmy * 1024 + dmo * 128:dmy * 1024 + (dmo + 1) * 128],
                                           ysrc[:, dmy * 256:(dmy + 1) * 256],
                                           start=(dmy == 0), stop=(dmy == 7))
                    return ins
                ch.step("tensor", _proj)
                ch.step("vector", lambda v: v.tensor_copy(sqb, pS[:, 0:2048]))

                for qi in range(2):
                    def _otr(t, qi=qi):
                        for dmo in range(8):
                            ins = t.transpose(pA[:, dmo * 128:(dmo + 1) * 128],
                                              sqb[:, dmo * 256 + qi * 128:dmo * 256 + (qi + 1) * 128], ident[:])
                        return ins
                    ch.step("tensor", _otr)

                    def _oad(v, g=g, qi=qi):
                        return resadd(v, 2 * g + qi, pA[:, 0:1024], 1024, P1[:, 7440:8464])
                    ch.step("vector", _oad)

        def emit_ffn_l0():
            acc0 = st16[:, 240:242]

            def _wsum_init(v):
                v.memset(st8[:], 0.0)
                return v.memset(st16[:, 224:256], 0.0)
            ch.step("vector", _wsum_init)
            for wi, wd in enumerate((w1T_d, w2T_d)):
                for c in range(8):
                    if wi == 0:
                        def _ws(sy, wd=wd, c=c):
                            sy.dma_start(out=P1[:, 0:4096].rearrange("p (a c) -> p a c", c=512),
                                         in_=wd.rearrange("(a p) c -> p a c", p=128)[:, :, c * 512:(c + 1) * 512]).then_inc(SY, 16)
                    else:
                        def _ws(sy, wd=wd, c=c):
                            sy.dma_start(out=P1[:, 0:4096].rearrange("p (a c) -> p a c", c=1024),
                                         in_=wd.rearrange("(a p) c -> p a c", p=128)[:, c * 4:(c + 1) * 4, :]).then_inc(SY, 16)
                    ch.step("sync", _ws, dmas=1)

                    def _wsr(v, wi=wi, c=c):
                        acc = st16[:, 224 + 8 * wi:232 + 8 * wi]
                        v.tensor_scalar(P2[:, 0:4096], P1[:, 0:4096], -1.0, None, ALU.mult)
                        v.tensor_max(P1[:, 4096:8192], P1[:, 0:4096], P2[:, 0:4096])
                        ins = v.reduce_sum(st8[:, 7:8], P1[:, 4096:8192], axis=AXL.X)
                        vf(v, ins)
                        vf(v, v.tensor_add(st8[:, 8:16], st8[:, 0:8], acc))
                        return v.tensor_scalar(acc, st8[:, 8:16], 1.0, None, ALU.mult)
                    ch.step("vector", _wsr)

            def _wsfin(t):
                return t.matmul(pC[0:1, 0:16], ones_col[:, 0:1], st16[:, 224:240], start=True, stop=True)
            ch.step("tensor", _wsfin)

            def _wsca(v):
                vf(v, v.tensor_scalar(rowB[0:1, 0:16], pC[0:1, 0:16], float(1.0 / (FI * D)), None, ALU.mult))
                vf(v, v.tensor_scalar(rowB[0:1, 16:32], rowB[0:1, 0:16], 1e-5, None, ALU.max))
                return v.reciprocal(rowB[0:1, 32:48], rowB[0:1, 16:32])
            ch.step("vector", _wsca)
            ch.step("tensor", lambda t: t.matmul(pC[:, 0:48], ones_row[:], rowB[0:1, 0:48], start=True, stop=True))
            ch.step("vector", lambda v: v.tensor_scalar(st16[:, 0:48], pC[:, 0:48], 1.0, None, ALU.mult))
            tst = tap("st16b", [128, 256])
            if tst is not None:
                def _tst(sy):
                    sy.dma_start(out=tst, in_=st16[:]).then_inc(SY, 16)
                ch.step("sync", _tst, dmas=1)

            def _rx1(sy):
                sy.dma_start(out=row_bnc, in_=st16[:, 96:112]).then_inc(SY, 16)
            ch.step("sync", _rx1, dmas=1)

            def _rx2(sy):
                sy.dma_start(out=rowA[0:1, :], in_=row_bnc.transpose([1, 0]),
                             allow_slow_non_contiguous=True).then_inc(SY, 16)
            ch.step("sync", _rx2, dmas=1)

            amax = P2[:, 2048:4096]

            def _am0(v):
                return v.memset(amax, 0.0)
            ch.step("vector", _am0)

            for c in range(8):
                def _w1ld(sy, c=c):
                    sy.dma_start(out=P1[:, 0:4096].rearrange("p (a c) -> p a c", c=512),
                                 in_=w1T_d.rearrange("(a p) c -> p a c", p=128)[:, :, c * 512:(c + 1) * 512]).then_inc(SY, 16)
                    sy.dma_start(out=st16[:, 224:236].rearrange("p (a c) -> p a c", c=3),
                                 in_=convw_d.rearrange("(a p) c -> p a c", p=128)[:, c * 4:(c + 1) * 4, :]).then_inc(SY, 16)
                    sy.dma_start(out=st16[:, 236:240].rearrange("p (a c) -> p a c", c=1),
                                 in_=convb_d.rearrange("(a p) c -> p a c", p=128)[:, c * 4:(c + 1) * 4, :],
                                 allow_slow_non_contiguous=True).then_inc(SY, 16)
                ch.step("sync", _w1ld, dmas=3)
                if c == 0:
                    twl = tap("w1ld", [128, 4096])
                    if twl is not None:
                        def _twl(sy):
                            sy.dma_start(out=twl, in_=P1[:, 0:4096]).then_inc(SY, 16)
                        ch.step("sync", _twl, dmas=1)

                def _w1qa(v, c=c):
                    v.tensor_scalar(st8[:, 32:40], st16[:, 32:40], 1.0, None, ALU.mult)
                    v.tensor_mul(P1[:, 4096:6144], P1[:, 0:2048], st8[:, 39:40].broadcast_to((128, 2048)))
                    return v.tensor_mul(P1[:, 6144:8192], P1[:, 2048:4096], st8[:, 39:40].broadcast_to((128, 2048)))
                ch.step("vector", _w1qa)
                if c == 0:
                    tmu = tap("w1mul", [128, 4096])
                    if tmu is not None:
                        def _tmu(sy):
                            sy.dma_start(out=tmu, in_=P1[:, 4096:8192]).then_inc(SY, 16)
                        ch.step("sync", _tmu, dmas=1)
                    ts8 = tap("st8s", [128, 40])
                    if ts8 is not None:
                        def _ts8(sy):
                            sy.dma_start(out=ts8, in_=st8[:]).then_inc(SY, 16)
                        ch.step("sync", _ts8, dmas=1)

                def _w1q(v, c=c):
                    for hh in range(4):
                        rne(v, P1[:, hh * 1024:(hh + 1) * 1024], P1[:, 4096 + hh * 1024:4096 + (hh + 1) * 1024],
                            P2[:, 0:1024], P2[:, 1024:2048])
                    v.tensor_scalar(P1[:, 4096:8192], P1[:, 0:4096], 1.0, -1.0, ALU.min, ALU.max)
                    return v.tensor_copy(WB[:, 0:4096], P1[:, 4096:8192])
                ch.step("vector", _w1q)

                if c == 0:
                    twr = tap("w1raw", [128, 4096])
                    if twr is not None:
                        def _twr(sy):
                            sy.dma_start(out=twr, in_=P1[:, 0:4096]).then_inc(SY, 16)
                        ch.step("sync", _twr, dmas=1)
                    tw1 = tap("w1qs", [128, 4096], BF16)
                    if tw1 is not None:
                        def _tw1(sy):
                            sy.dma_start(out=tw1, in_=WB[:, 0:4096]).then_inc(SY, 16)
                        ch.step("sync", _tw1, dmas=1)
                for tc_ in range(4):
                    def _qxld(sy, tc_=tc_):
                        sy.dma_start(out=WB[:, 4096:8192].rearrange("p (a c) -> p a c", c=512),
                                     in_=qxT_park.rearrange("(a p) t -> p a t", p=128)[:, :, tc_ * 512:(tc_ + 1) * 512]).then_inc(SY, 16)
                    ch.step("sync", _qxld, dmas=1)

                    for fi in range(4):
                        def _w1mm(t, fi=fi, tc_=tc_):
                            for kd in range(8):
                                ins = t.matmul(pB[:, 0:512],
                                               WB[:, fi * 128 + 0:0] if False else WB[:, 0:4096].rearrange("p (a c) -> p a c", c=512)[:, kd, fi * 128:(fi + 1) * 128],
                                               WB[:, 4096:8192].rearrange("p (a c) -> p a c", c=512)[:, kd, :],
                                               start=(kd == 0), stop=(kd == 7))
                            return ins
                        ch.step("tensor", _w1mm)

                        def _hev(v, fi=fi, tc_=tc_, c=c):
                            return v.tensor_copy(WP[:, fi * 2050 + 1 + tc_ * 512:fi * 2050 + 1 + tc_ * 512 + 512], pB[:, 0:512])
                        ch.step("vector", _hev)

                for fi in range(4):
                    if fi == 0:
                        ch.step("tensor", lambda t: t.matmul(pS[:, 0:512], ones_row[:], rowA[0:1, 0:512], start=True, stop=True))
                        ch.step("tensor", lambda t: t.matmul(pS[:, 512:1024], ones_row[:], rowA[0:1, 512:1024], start=True, stop=True))
                        ch.step("tensor", lambda t: t.matmul(pS[:, 1024:1536], ones_row[:], rowA[0:1, 1024:1536], start=True, stop=True))
                        ch.step("tensor", lambda t: t.matmul(pS[:, 1536:2048], ones_row[:], rowA[0:1, 1536:2048], start=True, stop=True))

                    def _hsc(v, fi=fi):
                        hs = WP[:, fi * 2050:(fi + 1) * 2050]
                        v.memset(hs[:, 0:1], 0.0)
                        vf(v, v.memset(hs[:, 2049:2050], 0.0))
                        vf(v, v.tensor_copy(st8[:, 32:40], st16[:, 16:24]))
                        v.tensor_mul(P1[:, 0:2048], hs[:, 1:2049], pS[:, 0:2048])
                        return v.tensor_mul(hs[:, 1:2049], P1[:, 0:2048], st8[:, 39:40].broadcast_to((128, 2048)))
                    ch.step("vector", _hsc)

                    def _cnv(v, fi=fi):
                        hs = WP[:, fi * 2050:(fi + 1) * 2050]
                        w0 = st16[:, 224 + fi * 3:225 + fi * 3]
                        w1c = st16[:, 225 + fi * 3:226 + fi * 3]
                        w2c = st16[:, 226 + fi * 3:227 + fi * 3]
                        v.tensor_mul(P1[:, 0:2048], hs[:, 0:2048], w0.broadcast_to((128, 2048)))
                        v.tensor_mul(P1[:, 2048:4096], hs[:, 1:2049], w1c.broadcast_to((128, 2048)))
                        v.tensor_add(P1[:, 4096:6144], P1[:, 0:2048], P1[:, 2048:4096])
                        v.tensor_mul(P1[:, 0:2048], hs[:, 2:2050], w2c.broadcast_to((128, 2048)))
                        return v.tensor_add(P1[:, 2048:4096], P1[:, 4096:6144], P1[:, 0:2048])
                    ch.step("vector", _cnv)

                    def _silu(s, fi=fi):
                        return s.activation(P1[:, 4096:6144], P1[:, 2048:4096], ACT.Silu,
                                            bias=st16[:, 236 + fi:237 + fi])
                    ch.step("scalar", _silu)

                    def _amx(v, fi=fi):
                        v.tensor_scalar(P1[:, 0:2048], P1[:, 4096:6144], -1.0, None, ALU.mult)
                        v.tensor_max(P1[:, 2048:4096], P1[:, 4096:6144], P1[:, 0:2048])
                        v.tensor_max(P1[:, 0:2048], amax, P1[:, 2048:4096])
                        return v.tensor_copy(amax, P1[:, 0:2048])
                    ch.step("vector", _amx)

                    def _hpk(sy, c=c, fi=fi):
                        sy.dma_start(out=h_park[c * 512 + fi * 128:c * 512 + (fi + 1) * 128, :],
                                     in_=P1[:, 4096:6144]).then_inc(SY, 16)
                    ch.step("sync", _hpk, dmas=1)
                if c == 0:
                    thw = tap("hpre", [128, 2050])
                    if thw is not None:
                        def _thw(sy):
                            sy.dma_start(out=thw, in_=WP[:, 0:2050]).then_inc(SY, 16)
                        ch.step("sync", _thw, dmas=1)

            def _amtr(t):
                for i in range(16):
                    ins = t.transpose(pS[:, i * 128:(i + 1) * 128], amax[:, i * 128:(i + 1) * 128], ident[:])
                return ins
            ch.step("tensor", _amtr)

            def _amr(v):
                ins = v.reduce_max(st16[:, 224:240].rearrange("p (g o) -> p g o", o=1),
                             pS[:, 0:2048].rearrange("p (a c) -> p a c", c=128), axis=AXL.X)
                vf(v, ins)
                vf(v, v.tensor_scalar(st16[:, 240:256], st16[:, 224:240], 1e-5, None, ALU.max))
                vf(v, v.reciprocal(st16[:, 224:240], st16[:, 240:256]))
                v.tensor_scalar(st16[:, 208:224], st16[:, 224:240], 127.0, None, ALU.mult)
                ins = v.tensor_scalar(st16[:, 192:208], st16[:, 240:256], float(1.0 / 127.0), None, ALU.mult)
                vf(v, ins)
                v.tensor_scalar(st8[:, 32:40], st16[:, 24:32], 1.0, None, ALU.mult)
                return v.tensor_mul(st16[:, 176:192], st16[:, 192:208], st8[:, 39:40].broadcast_to((128, 16)))
            ch.step("vector", _amr)

            def _sh1(sy):
                sy.dma_start(out=row_bnc, in_=st16[:, 208:224]).then_inc(SY, 16)
            ch.step("sync", _sh1, dmas=1)

            def _sh2(sy):
                sy.dma_start(out=rowA[0:1, :], in_=row_bnc.transpose([1, 0]),
                             allow_slow_non_contiguous=True).then_inc(SY, 16)
            ch.step("sync", _sh2, dmas=1)

            def _sh3(sy):
                sy.dma_start(out=row_bnc, in_=st16[:, 176:192]).then_inc(SY, 16)
            ch.step("sync", _sh3, dmas=1)

            def _sh4(sy):
                sy.dma_start(out=rowC[0:1, :], in_=row_bnc.transpose([1, 0]),
                             allow_slow_non_contiguous=True).then_inc(SY, 16)
            ch.step("sync", _sh4, dmas=1)

            for c in range(8):
                def _w2ld(sy, c=c):
                    sy.dma_start(out=P1[:, 0:4096].rearrange("p (a c) -> p a c", c=1024),
                                 in_=w2T_d.rearrange("(a p) c -> p a c", p=128)[:, c * 4:(c + 1) * 4, :]).then_inc(SY, 16)
                ch.step("sync", _w2ld, dmas=1)

                def _w2q(v, c=c):
                    v.tensor_scalar(st8[:, 32:40], st16[:, 40:48], 1.0, None, ALU.mult)
                    v.tensor_mul(P1[:, 4096:6144], P1[:, 0:2048], st8[:, 39:40].broadcast_to((128, 2048)))
                    v.tensor_mul(P1[:, 6144:8192], P1[:, 2048:4096], st8[:, 39:40].broadcast_to((128, 2048)))
                    for hh in range(4):
                        rne(v, P1[:, hh * 1024:(hh + 1) * 1024], P1[:, 4096 + hh * 1024:4096 + (hh + 1) * 1024],
                            P2[:, 0:1024], P2[:, 1024:2048])
                    v.tensor_scalar(P1[:, 4096:8192], P1[:, 0:4096], 1.0, -1.0, ALU.min, ALU.max)
                    return v.tensor_copy(WB[:, 0:4096], P1[:, 4096:8192])
                ch.step("vector", _w2q)

                for tc_ in range(4):
                    def _hld(sy, c=c, tc_=tc_):
                        sy.dma_start(out=P1[:, 4096:6144].rearrange("p (a c) -> p a c", c=512),
                                     in_=h_park[c * 512:(c + 1) * 512, tc_ * 512:(tc_ + 1) * 512].rearrange(
                                         "(a p) t -> p a t", p=128)).then_inc(SY, 16)
                    ch.step("sync", _hld, dmas=1)

                    def _bcs(t, tc_=tc_):
                        return t.matmul(pC[:, 0:512], ones_row[:], rowA[0:1, tc_ * 512:(tc_ + 1) * 512],
                                        start=True, stop=True)
                    ch.step("tensor", _bcs)

                    def _qh(v, c=c, tc_=tc_):
                        v.tensor_mul(P1[:, 6144:8192].rearrange("p (a c) -> p a c", c=512),
                                     P1[:, 4096:6144].rearrange("p (a c) -> p a c", c=512),
                                     pC[:, 0:512].unsqueeze(1).broadcast_to((128, 4, 512)))
                        rne(v, P1[:, 0:1024], P1[:, 6144:7168], P2[:, 0:1024], P2[:, 1024:2048])
                        rne(v, P1[:, 1024:2048], P1[:, 7168:8192], P2[:, 0:1024], P2[:, 1024:2048])
                        return v.tensor_copy(WB[:, 4096:6144], P1[:, 0:2048])
                    ch.step("vector", _qh)

                    for half in range(2):
                        def _w2mmh(t, c=c, tc_=tc_, half=half):
                            for dmo in range(8):
                                for kd in range(4):
                                    ins = t.matmul(pS[:, dmo * 256:dmo * 256 + 256],
                                                   WB[:, kd * 1024 + dmo * 128:kd * 1024 + (dmo + 1) * 128],
                                                   WB[:, 4096 + kd * 512 + half * 256:4096 + kd * 512 + half * 256 + 256],
                                                   start=(kd == 0), stop=(kd == 3))
                            return ins
                        ch.step("tensor", _w2mmh)

                        def _fold(t, tc_=tc_, half=half):
                            return t.matmul(pC[:, 0:256], ones_row[:],
                                            rowC[0:1, tc_ * 512 + half * 256:tc_ * 512 + half * 256 + 256],
                                            start=True, stop=True)
                        ch.step("tensor", _fold)

                        ch.step("vector", lambda v: v.tensor_copy(P1[:, 2048:2304], pC[:, 0:256]))

                        def _fev(v, half=half):
                            return v.tensor_mul(P1[:, 0:2048].rearrange("p (a c) -> p a c", c=256),
                                                pS[:, 0:2048].rearrange("p (a c) -> p a c", c=256),
                                                P1[:, 2048:2304].unsqueeze(1).broadcast_to((128, 8, 256)))
                        ch.step("vector", _fev)

                        for qh_ in range(2):
                            def _w2tr(t, qh_=qh_):
                                for dmo in range(8):
                                    ins = t.transpose(pA[:, dmo * 128:(dmo + 1) * 128],
                                                      P1[:, dmo * 256 + qh_ * 128:dmo * 256 + (qh_ + 1) * 128],
                                                      ident[:])
                                return ins
                            ch.step("tensor", _w2tr)

                            def _w2ad(v, tc_=tc_, half=half, qh_=qh_):
                                qg = tc_ * 4 + half * 2 + qh_
                                return resadd(v, qg, pA[:, 0:1024], 1024, P1[:, 8192:9216])
                            ch.step("vector", _w2ad)

        def emit_ffn_l1():
            for ic in range(NIC):
                icw = IC_W[ic]
                ico = IC_OFF[ic]
                na = icw // 128

                def _wgl(sy, ico=ico, icw=icw, na=na):
                    sy.dma_start(out=P1[:, 0:na * 1024].rearrange("p (a c) -> p a c", c=1024),
                                 in_=wgate_d.rearrange("(a p) c -> p a c", p=128)[:, ico // 128:ico // 128 + na, :]).then_inc(SY, 16)
                    sy.dma_start(out=P1[:, 4096:4096 + na * 1024].rearrange("p (a c) -> p a c", c=1024),
                                 in_=wup_d.rearrange("(a p) c -> p a c", p=128)[:, ico // 128:ico // 128 + na, :]).then_inc(SY, 16)
                    sy.dma_start(out=P2[:, 0:8 * icw].rearrange("p (a c) -> p a c", c=icw) if icw <= 512 else None,
                                 in_=wdown_d.rearrange("(a p) c -> p a c", p=128)[:, :, ico:ico + icw]).then_inc(SY, 16)
                ch.step("sync", _wgl, dmas=3)

                tot_g = na * 1024
                for (off, tot) in ((0, tot_g), (4096, tot_g)):
                    for hh in range((tot + 2047) // 2048):
                        w0 = off + hh * 2048
                        wd_ = min(off + tot, w0 + 2048) - w0
                        nb = wd_ // 16

                        def _fq(v, w0=w0, wd_=wd_, nb=nb):
                            fp4q(v, WP[:, 8192:8192 + wd_], P1[:, w0:w0 + wd_],
                                 WP[:, 0:wd_], WP[:, 2048:2048 + wd_], WP[:, 4096:4096 + wd_],
                                 st16[:, 0:nb], st16[:, 128:128 + nb], nblk=nb)
                            return v.tensor_copy(P1[:, w0:w0 + wd_], WP[:, 8192:8192 + wd_])
                        ch.step("vector", _fq)

                for mi, off in enumerate((0, 4096)):
                    for a in range(na):
                        def _wtr(t, a=a, off=off):
                            for dd in range(8):
                                ins = t.transpose(pS[:, dd * 128:(dd + 1) * 128],
                                                  P1[:, off + a * 1024 + dd * 128:off + a * 1024 + (dd + 1) * 128],
                                                  ident[:])
                            return ins
                        ch.step("tensor", _wtr)

                        def _wte(v, a=a, mi=mi, icw=icw):
                            return v.tensor_copy(
                                WP[:, mi * 4096:mi * 4096 + 8 * icw].rearrange("p (d c) -> p d c", c=icw)[:, :, a * 128:(a + 1) * 128],
                                pS[:, 0:1024].rearrange("p (d c) -> p d c", c=128))
                        ch.step("vector", _wte)
                tot_d = 8 * icw
                for hh in range((tot_d + 2047) // 2048):
                    w0 = hh * 2048
                    wd_ = min(tot_d, w0 + 2048) - w0
                    nb = wd_ // 16

                    def _fqd(v, w0=w0, wd_=wd_, nb=nb):
                        fp4q(v, P1[:, 0:wd_], P2[:, w0:w0 + wd_],
                             P1[:, 2048:2048 + wd_], P1[:, 4096:4096 + wd_], P1[:, 6144:6144 + wd_],
                             st16[:, 0:nb], st16[:, 128:128 + nb], nblk=nb)
                        return v.tensor_copy(P2[:, w0:w0 + wd_], P1[:, 0:wd_])
                    ch.step("vector", _fqd)

                for dd in range(8):
                    def _dtr(t, dd=dd, icw=icw, na=na):
                        for a in range(na):
                            ins = t.transpose(pS[:, a * 128:(a + 1) * 128],
                                              P2[:, dd * icw + a * 128:dd * icw + (a + 1) * 128],
                                              ident[:])
                        return ins
                    ch.step("tensor", _dtr)

                    def _dte(v, dd=dd, na=na):
                        return v.tensor_copy(
                            WP[:, 8192:8192 + na * 1024].rearrange("p (a c) -> p a c", c=1024)[:, :, dd * 128:(dd + 1) * 128],
                            pS[:, 0:na * 128].rearrange("p (a c) -> p a c", c=128))
                    ch.step("vector", _dte)

                for ti in range(NT):
                    def _xql(sy, ti=ti):
                        sy.dma_start(out=P1[:, 0:1024].rearrange("p (a c) -> p a c", c=128),
                                     in_=xqT_park.rearrange("(a p) t -> p a t", p=128)[:, :, ti * 128:(ti + 1) * 128]).then_inc(SY, 16)
                    ch.step("sync", _xql, dmas=1)

                    def _gmm(t, icw=icw):
                        for kd in range(8):
                            ins = t.matmul(pB[:, 0:icw], P1[:, kd * 128:(kd + 1) * 128],
                                           WP[:, kd * icw:(kd + 1) * icw],
                                           start=(kd == 0), stop=(kd == 7))
                        return ins
                    ch.step("tensor", _gmm)

                    def _umm(t, icw=icw):
                        for kd in range(8):
                            ins = t.matmul(pC[:, 0:icw], P1[:, kd * 128:(kd + 1) * 128],
                                           WP[:, 4096 + kd * icw:4096 + (kd + 1) * icw],
                                           start=(kd == 0), stop=(kd == 7))
                        return ins
                    ch.step("tensor", _umm)

                    ch.step("scalar", lambda s, icw=icw: s.activation(P1[:, 1024:1024 + icw], pB[:, 0:icw], ACT.Silu))

                    def _h1(v, icw=icw):
                        return v.tensor_mul(P1[:, 2048:2048 + icw], P1[:, 1024:1024 + icw], pC[:, 0:icw])
                    ch.step("vector", _h1)

                    def _h1q(v, icw=icw):
                        nb = icw // 16
                        return fp4q(v, P1[:, 6144:6144 + icw], P1[:, 2048:2048 + icw],
                                    P1[:, 3072:3072 + icw], P1[:, 4096:4096 + icw], P1[:, 5120:5120 + icw],
                                    st16[:, 0:nb], st16[:, 128:128 + nb], nblk=nb)
                    ch.step("vector", _h1q)

                    def _htr(t, icw=icw, na=na):
                        for a in range(na):
                            ins = t.transpose(pA[:, a * 128:(a + 1) * 128],
                                              P1[:, 6144 + a * 128:6144 + (a + 1) * 128], ident[:])
                        return ins
                    ch.step("tensor", _htr)

                    def _hte(v, na=na):
                        return v.tensor_copy(P1[:, 7168:7168 + na * 128], pA[:, 0:na * 128])
                    ch.step("vector", _hte)

                    def _dmm(t, na=na):
                        for a in range(na):
                            for n0 in (0, 512):
                                ins = t.matmul(pS[:, n0:n0 + 512], P1[:, 7168 + a * 128:7168 + (a + 1) * 128],
                                               WP[:, 8192 + a * 1024 + n0:8192 + a * 1024 + n0 + 512],
                                               start=(a == 0), stop=(a == na - 1))
                        return ins
                    ch.step("tensor", _dmm)

                    def _dadd(v, ti=ti):
                        return resadd(v, ti, pS[:, 0:1024], 1024, P1[:, 8192:9216])
                    ch.step("vector", _dadd)

        if upto >= 2:
            emit_ln_prep(0)
            emit_l0_actscales()
        if upto >= 2.1:
            emit_ternary_wqkv()
        if upto >= 2.2:
            emit_qkv_l0()
        if upto >= 2.3:
            emit_ternary_wproj()
        if upto >= 2.4:
            emit_attn(0)
        if upto >= 3:
            emit_ffn_l0()
        if upto >= 4:
            emit_ln_prep(1)
            emit_qkv_l1()

            def _wol(sy):
                sy.dma_start(out=WP[:, 0:8192].rearrange("p (a c) -> p a c", c=1024),
                             in_=wo1T_d.rearrange("(a p) c -> p a c", p=128)).then_inc(SY, 16)
            ch.step("sync", _wol, dmas=1)
            emit_attn(1)
        if upto >= 5:
            emit_ffn_l1()

        twp = tap("wps", [128, 12288])
        if twp is not None:
            def _twp(sy):
                sy.dma_start(out=twp, in_=WP[:]).then_inc(SY, 16)
            ch.step("sync", _twp, dmas=1)
        tqx = tap("qxs", [D, T], BF16)
        if tqx is not None:
            def _tqx(sy):
                sy.dma_start(out=tqx, in_=qxT_park).then_inc(SY, 16)
            ch.step("sync", _tqx, dmas=1)
        tra = tap("rowAs", [1, T])
        if tra is not None:
            def _tra(sy):
                sy.dma_start(out=tra, in_=rowA[:]).then_inc(SY, 16)
            ch.step("sync", _tra, dmas=1)
        trb = tap("rowBs", [1, 256])
        if trb is not None:
            def _trb(sy):
                sy.dma_start(out=trb, in_=rowB[:]).then_inc(SY, 16)
            ch.step("sync", _trb, dmas=1)
        thp = tap("hps", [FI, T])
        if thp is not None:
            def _thp(sy):
                sy.dma_start(out=thp, in_=h_park).then_inc(SY, 16)
            ch.step("sync", _thp, dmas=1)
        t16 = tap("st16s", [128, 256])
        if t16 is not None:
            def _t16(sy):
                sy.dma_start(out=t16, in_=st16[:]).then_inc(SY, 16)
            ch.step("sync", _t16, dmas=1)
        t15 = tap("xnT1s", [128, D])
        if t15 is not None:
            def _t15(sy):
                sy.dma_start(out=t15, in_=P1[:, 5120:6144]).then_inc(SY, 16)
            ch.step("sync", _t15, dmas=1)
        t15b = tap("xn1s", [128, D])
        if t15b is not None:
            def _t15b(sy):
                sy.dma_start(out=t15b, in_=P1[:, 0:1024]).then_inc(SY, 16)
            ch.step("sync", _t15b, dmas=1)
        for pname, park, rows in (("xnpark", xnT_park, D), ("qpark", qT_park, D),
                                  ("kpark", kT_park, D), ("vpark", v_park, T)):
            pt = tap(pname, [rows, 1040 if pname == "vpark" else T])
            if pt is not None:
                def _pt(sy, pt=pt, park=park):
                    sy.dma_start(out=pt, in_=park).then_inc(SY, 16)
                ch.step("sync", _pt, dmas=1)

        xr_tap = tap("x_res", [128, NT * D])
        if xr_tap is not None:
            def _xrt(sy):
                sy.dma_start(out=xr_tap, in_=x_res[:]).then_inc(SY, 16)
            ch.step("sync", _xrt, dmas=1)

        def _store(sy):
            sy.dma_start(out=y_out.rearrange("(t p) d -> p t d", p=128),
                         in_=x_res[:].rearrange("p (t d) -> p t d", d=D)).then_inc(SY, 16)
        ch.step("sync", _store, dmas=1)

        ch.emit(fence_proxy=FenceProxy)
        with nc.Block() as blk:
            @blk.sync
            def _(sy):
                sy.wait_ge(SY, ch.cnt["sync"])
    return nc, dbg


def host_tables():
    import jax
    cpu = jax.devices("cpu")[0]
    import jax.numpy as jnp
    with jax.default_device(cpu):
        inv_freq = 1.0 / jnp.asarray(10000.0, jnp.float32) ** (
            jnp.arange(0, 64, 2, dtype=jnp.float32) / 64)
        freqs = jnp.outer(jnp.arange(T, dtype=jnp.float32), inv_freq)
        cos0 = np.asarray(jnp.cos(freqs), np.float32)
        sin0 = np.asarray(jnp.sin(freqs), np.float32)
        ang1 = np.asarray(jnp.power(jnp.asarray(10000.0, jnp.float32),
                                    2.0 * jnp.arange(32, dtype=jnp.float32) / 64), np.float32)
    return cos0, sin0, np.ascontiguousarray(ang1.reshape(1, 32))


def kernel(L0_attn_x, L0_attn_w_qkv, L0_attn_w_proj, L0_attn_q_gain,
           L0_ffn_w1, L0_ffn_conv_w, L0_ffn_conv_b, L0_ffn_w2,
           L1_attn_w_q, L1_attn_w_k, L1_attn_w_v, L1_attn_w_o,
           L1_ffn_w_gate, L1_ffn_w_up, L1_ffn_w_down, _debug_taps=(), _upto=99):
    cos0, sin0, ang1 = host_tables()
    asc = np.ascontiguousarray
    base = {
        "w_qkv": asc(np.asarray(L0_attn_w_qkv, np.float32)),
        "w_proj": asc(np.asarray(L0_attn_w_proj, np.float32)),
        "q_gain": asc(np.asarray(L0_attn_q_gain, np.float32).reshape(1, 16)),
        "w1T": asc(np.asarray(L0_ffn_w1, np.float32).T),
        "conv_w": asc(np.asarray(L0_ffn_conv_w, np.float32).reshape(FI, 3)),
        "conv_b": asc(np.asarray(L0_ffn_conv_b, np.float32).reshape(FI, 1)),
        "w2T": asc(np.asarray(L0_ffn_w2, np.float32).T),
        "wq1T": asc(np.asarray(L1_attn_w_q, np.float32).T),
        "wk1T": asc(np.asarray(L1_attn_w_k, np.float32).T),
        "wv1T": asc(np.asarray(L1_attn_w_v, np.float32).T),
        "wo1T": asc(np.asarray(L1_attn_w_o, np.float32).T),
        "w_gate": asc(np.asarray(L1_ffn_w_gate, np.float32)),
        "w_up": asc(np.asarray(L1_ffn_w_up, np.float32)),
        "w_down": asc(np.asarray(L1_ffn_w_down, np.float32)),
        "cos0": cos0, "sin0": sin0, "ang1": ang1,
    }
    x = np.asarray(L0_attn_x, np.float32)
    nc, dbg = build_nc(_debug_taps, upto=_upto)
    in_maps = [dict(base, x=asc(x[b])) for b in range(8)]
    res = run_bass_kernel_spmd(nc, in_maps, core_ids=list(range(8)))
    out = np.stack([np.asarray(res.results[b]["y"]) for b in range(8)], axis=0)
    if _debug_taps:
        return out, [{k: np.asarray(r["dbg_" + k]) for k in _debug_taps} for r in res.results]
    return out


# version 02 (final)
# speedup vs baseline: 1.0155x; 1.0155x over previous
import contextlib
import numpy as np

import concourse.bass as bass
import concourse.mybir as mybir
from concourse.bass_utils import run_bass_kernel_spmd

F32 = mybir.dt.float32
BF16 = mybir.dt.bfloat16
I32 = mybir.dt.int32
ALU = mybir.AluOpType
ACT = mybir.ActivationFunctionType
AXL = mybir.AxisListType

C32 = float(1.5 * 2 ** 23)
C64 = float(1.5 * 2 ** 52)
T = 2048
D = 1024
NT = 16
EPS32 = float(2.0 ** -23)
EPS64 = float(2.0 ** -52)
FI = 4096
I1 = 2816
NIC = 6
IC_OFF = [0, 512, 1024, 1536, 2048, 2560]
IC_W = [512, 512, 512, 512, 512, 256]


def _pi_chunks(n=5, bits=11):
    import math
    rem = np.float64(math.pi)
    out = []
    for _ in range(n):
        if rem == 0:
            out.append(0.0)
            continue
        e = int(np.floor(np.log2(abs(rem))))
        sc = 2.0 ** (e - bits + 1)
        c = np.float64(np.round(rem / sc) * sc)
        out.append(float(np.float32(c)))
        rem = np.float64(rem) - np.float64(np.float32(c))
    return out


PI_CH = _pi_chunks()
PI_F32 = float(np.float32(np.pi))
PI_HALF = float(np.float32(np.pi / 2))
INV_PI = float(np.float32(1.0 / np.pi))
MIDS = [0.25, 0.75, 1.25, 1.75, 2.5, 3.5, 5.0]
BF16_1EM8 = 9.958720207214355e-09

ENGS = ("sync", "tensor", "vector", "scalar", "gpsimd")


class Chain:
    def __init__(self, nc, es):
        self.nc = nc
        self.sems = {e: es.enter_context(nc.semaphore("c_" + e)) for e in ENGS}
        self.cnt = {e: 0 for e in ENGS}
        self.steps = []
        self.last = None

    def step(self, eng, fn, dmas=0, after=None):
        prev = [self.last] if after is None else list(after)
        inc = 16 * dmas if dmas else 1
        self.cnt[eng] += inc
        self.steps.append((eng, fn, prev, dmas))
        self.last = (eng, self.cnt[eng])
        return self.last

    def emit(self, fence_proxy=None):
        per_eng = {e: [] for e in ENGS}
        for eng, fn, prev, dmas in self.steps:
            per_eng[eng].append((fn, prev, dmas))
        sems = self.sems
        with self.nc.Block() as blk:
            def mk(eng):
                def body(h):
                    hw = h
                    if fence_proxy is not None and eng in ("vector", "scalar"):
                        hw = fence_proxy(h)
                    real_dma = h.dma_start
                    cnt = [0]

                    def counting_dma(*a, **k):
                        cnt[0] += 1
                        return real_dma(*a, **k)
                    for fn, prev, dmas in per_eng[eng]:
                        byeng = {}
                        for p in prev:
                            if p is not None:
                                byeng[p[0]] = max(byeng.get(p[0], 0), p[1])
                        for pe, pc in byeng.items():
                            h.wait_ge(sems[pe], pc)
                        if dmas:
                            cnt[0] = 0
                            h.dma_start = counting_dma
                            fn(h)
                            h.dma_start = real_dma
                            assert cnt[0] == dmas, (
                                f"step on {eng} declared {dmas} dmas, emitted {cnt[0]}: {fn}")
                        else:
                            ins = fn(hw)
                            if hw is not h:
                                h.sem_inc(sems[eng], 1)
                            else:
                                ins.then_inc(sems[eng], 1)
                return body
            for e in ENGS:
                getattr(blk, e)(mk(e))


def build_nc(debug_taps=(), upto=99.0):
    taps = set(debug_taps)
    nc = bass.Bass("TRN2", target_bir_lowering=False)

    x_in = nc.dram_tensor("x", [T, D], F32, kind="ExternalInput").ap()
    y_out = nc.dram_tensor("y", [T, D], F32, kind="ExternalOutput").ap()
    w_qkv = nc.dram_tensor("w_qkv", [1536, D], F32, kind="ExternalInput").ap()
    w_proj = nc.dram_tensor("w_proj", [D, D], F32, kind="ExternalInput").ap()
    q_gain = nc.dram_tensor("q_gain", [1, 16], F32, kind="ExternalInput").ap()
    w1T_d = nc.dram_tensor("w1T", [D, FI], F32, kind="ExternalInput").ap()
    convw_d = nc.dram_tensor("conv_w", [FI, 3], F32, kind="ExternalInput").ap()
    convb_d = nc.dram_tensor("conv_b", [FI, 1], F32, kind="ExternalInput").ap()
    w2T_d = nc.dram_tensor("w2T", [FI, D], F32, kind="ExternalInput").ap()
    wq1T_d = nc.dram_tensor("wq1T", [D, D], F32, kind="ExternalInput").ap()
    wk1T_d = nc.dram_tensor("wk1T", [D, D], F32, kind="ExternalInput").ap()
    wv1T_d = nc.dram_tensor("wv1T", [D, D], F32, kind="ExternalInput").ap()
    wo1T_d = nc.dram_tensor("wo1T", [D, D], F32, kind="ExternalInput").ap()
    wgate_d = nc.dram_tensor("w_gate", [I1, D], F32, kind="ExternalInput").ap()
    wup_d = nc.dram_tensor("w_up", [I1, D], F32, kind="ExternalInput").ap()
    wdown_d = nc.dram_tensor("w_down", [D, I1], F32, kind="ExternalInput").ap()
    cos0_d = nc.dram_tensor("cos0", [T, 32], F32, kind="ExternalInput").ap()
    sin0_d = nc.dram_tensor("sin0", [T, 32], F32, kind="ExternalInput").ap()
    ang1_d = nc.dram_tensor("ang1", [1, 32], F32, kind="ExternalInput").ap()
    qT_park = nc.dram_tensor("qT_park", [D, T], F32, kind="Internal").ap()
    kT_park = nc.dram_tensor("kT_park", [D, T], F32, kind="Internal").ap()
    v_park = nc.dram_tensor("v_park", [T, 16 * 65], F32, kind="Internal").ap()
    xnT_park = nc.dram_tensor("xnT_park", [D, T], F32, kind="Internal").ap()
    qxT_park = nc.dram_tensor("qxT_park", [D, T], BF16, kind="Internal").ap()
    xqT_park = nc.dram_tensor("xqT_park", [D, T], F32, kind="Internal").ap()
    h_park = nc.dram_tensor("h_park", [FI, T], F32, kind="Internal").ap()
    row_bnc = nc.dram_tensor("row_bnc", [128, 16], F32, kind="Internal").ap()

    dbg = {}

    def tap(name, shape, dt=F32):
        if name in taps:
            dbg[name] = nc.dram_tensor("dbg_" + name, shape, dt, kind="ExternalOutput").ap()
            return dbg[name]
        return None

    es = contextlib.ExitStack()
    with es:
        x_res = es.enter_context(nc.sbuf_tensor("x_res", [128, NT * D], F32))
        P1 = es.enter_context(nc.sbuf_tensor("P1", [128, 9216], F32))
        P2 = es.enter_context(nc.sbuf_tensor("P2", [128, 4096], F32))
        WP = es.enter_context(nc.sbuf_tensor("WP", [128, 12288], F32))
        WB = es.enter_context(nc.sbuf_tensor("WB", [128, 8192], BF16))
        cos0 = es.enter_context(nc.sbuf_tensor("cos0s", [128, 512], F32))
        sin0 = es.enter_context(nc.sbuf_tensor("sin0s", [128, 512], F32))
        cos1 = es.enter_context(nc.sbuf_tensor("cos1s", [128, 512], F32))
        sin1 = es.enter_context(nc.sbuf_tensor("sin1s", [128, 512], F32))
        ones_row = es.enter_context(nc.sbuf_tensor("ones_row", [1, 128], F32))
        ones_col = es.enter_context(nc.sbuf_tensor("ones_col", [128, 4], F32))
        ident = es.enter_context(nc.sbuf_tensor("ident", [128, 128], F32))
        tril = es.enter_context(nc.sbuf_tensor("tril", [128, 128], BF16))
        flag = es.enter_context(nc.sbuf_tensor("flag", [128, 8], F32))
        epsc = es.enter_context(nc.sbuf_tensor("epsc", [128, 24], F32))
        gain16 = es.enter_context(nc.sbuf_tensor("gain16", [128, 16], F32))
        st16 = es.enter_context(nc.sbuf_tensor("st16", [128, 256], F32))
        st8 = es.enter_context(nc.sbuf_tensor("st8", [128, 40], F32))
        rowA = es.enter_context(nc.sbuf_tensor("rowA", [1, T], F32))
        rowB = es.enter_context(nc.sbuf_tensor("rowB", [1, 256], F32))
        rowC = es.enter_context(nc.sbuf_tensor("rowC", [1, T], F32))
        cpool = es.enter_context(nc.sbuf_tensor("cpool", [128, 8], F32))
        pS = es.enter_context(nc.psum_tensor("pS", [128, 2048], F32))
        pA = es.enter_context(nc.psum_tensor("pA", [128, 1024], F32))
        pB = es.enter_context(nc.psum_tensor("pB", [128, 512], F32))
        pC = es.enter_context(nc.psum_tensor("pC", [128, 512], F32))

        CV = {val: i for i, val in enumerate((PI_HALF, -1.0, 1e-5, 6.0, 10000.0, 0.125, 2.0, 127.0))}

        def cvec(val):
            return cpool[:, CV[val]:CV[val] + 1]

        ch = Chain(nc, es)
        SY = ch.sems["sync"]
        auxv = es.enter_context(nc.semaphore("auxv"))
        auxs = es.enter_context(nc.semaphore("auxs"))
        AUXN = [0]
        AUXSN = [0]
        _NOFENCE = {"wait_ge", "sem_inc", "dma_start", "then_inc"}

        class FenceProxy:

            def __init__(self, h):
                self._h = h
                self._sem = auxv if h is not None else None

            def __getattr__(self, name):
                attr = getattr(self._h, name)
                if not callable(attr) or name.startswith("_") or name in _NOFENCE:
                    return attr
                def wrapped(*a, **k):
                    ins = attr(*a, **k)
                    try:
                        from concourse.bass_core_rust import BassInstruction as _BI
                        is_ins = isinstance(ins, _BI)
                    except Exception:
                        is_ins = hasattr(ins, "then_inc")
                    if is_ins:
                        import concourse.mybir as _mb
                        if getattr(self._h, "engine", None) == _mb.EngineType.Activation:
                            AUXSN[0] += 1
                            ins.then_inc(auxs, 1)
                            self._h.wait_ge(auxs, AUXSN[0])
                        else:
                            AUXN[0] += 1
                            ins.then_inc(auxv, 1)
                            self._h.wait_ge(auxv, AUXN[0])
                    return ins
                return wrapped

        def vf(v, ins):
            return ins

        def fbc(p, f):
            return flag[0:p, 0:1].broadcast_to((p, f))

        def rne(v, dst, src, tmpA, tmpB):
            return v.tensor_scalar(dst, src, C32, -C32, ALU.add, ALU.add)

        def mulb(v, out2d, in2d, col, b, i):
            return v.tensor_mul(out2d.rearrange("p (b i) -> p b i", i=i),
                                in2d.rearrange("p (b i) -> p b i", i=i),
                                col.unsqueeze(2).broadcast_to((128, b, i)))

        def fp4q(v, dst, src, ta, tb, tc, scA, scB, nblk):
            s3 = src.rearrange("p (b i) -> p b i", i=16)
            ins = v.reduce_max(scA.rearrange("p (g o) -> p g o", o=1), s3, axis=AXL.X, apply_absolute_value=True)
            vf(v, ins)
            ins = v.reciprocal(st8[:, 0:8], cpool[:, CV[6.0]:CV[6.0] + 1].broadcast_to((128, 8)))
            vf(v, ins)
            vf(v, v.tensor_mul(scB, scA, st8[:, 0:1].broadcast_to((128, nblk))))
            vf(v, v.tensor_scalar(scA, scB, 1e-12, None, ALU.max))
            vf(v, v.reciprocal(scB, scA))
            v.tensor_scalar(tc, src, -1.0, None, ALU.mult)
            v.tensor_max(ta, src, tc)
            mulb(v, tb, ta, scB, nblk, 16)
            v.memset(ta, 0.0)
            for m in MIDS:
                v.tensor_single_scalar(tc, tb, float(m), ALU.is_gt)
                v.tensor_copy(dst, ta)
                v.tensor_add(ta, dst, tc)
            v.tensor_scalar(tc, ta, -4.0, 0.0, ALU.add, ALU.max)
            v.tensor_scalar(dst, ta, 0.5, None, ALU.mult)
            v.tensor_scalar(tb, tc, 0.5, None, ALU.mult)
            v.tensor_add(tc, dst, tb)
            v.tensor_scalar(dst, ta, -6.0, 0.0, ALU.add, ALU.max)
            v.tensor_add(tb, tc, dst)
            v.tensor_single_scalar(ta, src, 0.0, ALU.is_lt)
            v.tensor_scalar(tc, ta, -2.0, 1.0, ALU.mult, ALU.add)
            v.tensor_mul(ta, tb, tc)
            return mulb(v, dst, ta, scA, nblk, 16)

        def resadd(v, qg, pAsrc, ncols, tmp):
            v.tensor_add(tmp, x_res[:, qg * D:qg * D + ncols], pAsrc)
            return v.tensor_copy(x_res[:, qg * D:qg * D + ncols], tmp)

        def _ld0(sy):
            sy.dma_start(out=x_res[:].rearrange("p (t d) -> p t d", d=D),
                         in_=x_in.rearrange("(t p) d -> p t d", p=128)).then_inc(SY, 16)
            sy.dma_start(out=cos0[:].rearrange("p (t c) -> p t c", c=32),
                         in_=cos0_d.rearrange("(t p) c -> p t c", p=128)).then_inc(SY, 16)
            sy.dma_start(out=sin0[:].rearrange("p (t c) -> p t c", c=32),
                         in_=sin0_d.rearrange("(t p) c -> p t c", p=128)).then_inc(SY, 16)
            sy.dma_start(out=rowA[0:1, 0:16], in_=q_gain).then_inc(SY, 16)
            sy.dma_start(out=rowA[0:1, 16:48], in_=ang1_d).then_inc(SY, 16)
        ch.step("sync", _ld0, dmas=5)

        def _iota0(g):
            g.iota(ident[:], [[1, 128]], base=0, channel_multiplier=-1,
                   allow_small_or_imprecise_dtypes=True)
            g.iota(st16[:, 128:256], [[1, 128]], base=0, channel_multiplier=-1,
                   allow_small_or_imprecise_dtypes=True)
            g.iota(st16[:, 0:16], [[128, 16]], base=0, channel_multiplier=1,
                   allow_small_or_imprecise_dtypes=True)
            return g.iota(rowB[0:1, 0:32], [[1, 32]], base=0, channel_multiplier=0,
                          allow_small_or_imprecise_dtypes=True)
        ch.step("gpsimd", _iota0)

        def _const0(v):
            v.tensor_single_scalar(ident[:], ident[:], 0.0, ALU.is_equal)
            v.tensor_single_scalar(tril[:], st16[:, 128:256], 0.0, ALU.is_le)
            v.memset(ones_row[:], 1.0)
            v.memset(ones_col[:], 1.0)
            for _val, _i in CV.items():
                v.memset(cpool[:, _i:_i + 1], _val)
            vf(v, v.tensor_scalar(flag[:], x_res[:, 0:8], 0.0, float(2.0 ** -40), ALU.mult, ALU.add))
            vf(v, v.tensor_scalar(st8[:, 0:8], flag[:], 1.0, -1.0, ALU.add, ALU.add))
            vf(v, v.tensor_scalar(flag[:], st8[:, 0:8], float(2.0 ** 40), None, ALU.mult))
            ins = v.tensor_scalar(st8[:, 0:8], flag[:], EPS64 - EPS32, None, ALU.mult)
            vf(v, ins)
            return v.tensor_single_scalar(epsc[:], st8[:, 0:1].broadcast_to((128, 24)), EPS32, ALU.add)
        ch.step("vector", _const0)

        ch.step("tensor", lambda t: t.matmul(pB[:, 0:16], ones_row[:], rowA[0:1, 0:16],
                                             start=True, stop=True))
        ch.step("vector", lambda v: v.tensor_scalar(gain16[:], pB[:, 0:16], 0.125, None, ALU.mult))

        TH_, KK_, K1_, K0_ = P1[:, 0:512], P1[:, 512:1024], P1[:, 1024:1536], P1[:, 1536:2048]
        RR_, SG_, T1_, T2_ = P1[:, 2048:2560], P1[:, 2560:3072], P1[:, 3072:3584], P1[:, 3584:4096]
        T3_, ANB = P1[:, 4096:4608], P1[:, 4608:5120]

        ch.step("vector", lambda v: v.tensor_scalar(rowB[0:1, 32:64], rowB[0:1, 0:32],
                                                    float(1.0 / 32.0), None, ALU.mult))

        ch.step("vector", lambda v: v.memset(rowC[0:1, 0:8], 10000.0))

        def _lnexp(s):
            s.activation(rowC[0:1, 8:16], rowC[0:1, 0:8], ACT.Ln)
            return s.activation(rowB[0:1, 64:96], rowB[0:1, 32:64], ACT.Exp, scale=rowC[0:1, 8:9])
        ch.step("scalar", _lnexp)

        def _angf(v):
            vf(v, v.tensor_sub(rowB[0:1, 96:128], rowB[0:1, 64:96], rowA[0:1, 16:48]))
            vf(v, v.tensor_mul(rowC[0:1, 8:40], rowB[0:1, 96:128], flag[0:1, 0:1].broadcast_to((1, 32))))
            return v.tensor_add(rowB[0:1, 128:160], rowC[0:1, 8:40], rowA[0:1, 16:48])
        ch.step("vector", _angf)

        ch.step("tensor", lambda t: t.matmul(pB[:, 0:32], ones_row[:], rowB[0:1, 128:160],
                                             start=True, stop=True))

        def _tbl(v):
            v.tensor_copy(ANB.rearrange("p (t i) -> p t i", i=32),
                          pB[:, 0:32].unsqueeze(1).broadcast_to((128, 16, 32)))
            v.tensor_mul(TH_.rearrange("p (t i) -> p t i", i=32),
                         ANB.rearrange("p (t i) -> p t i", i=32),
                         st16[:, 0:16].unsqueeze(2).broadcast_to((128, 16, 32)))
            v.tensor_scalar(T3_, TH_, INV_PI, None, ALU.mult)
            rne(v, KK_, T3_, T1_, T2_)
            v.tensor_scalar(T3_, KK_, float(2.0 ** -11), None, ALU.mult)
            rne(v, K1_, T3_, T1_, T2_)
            v.scalar_tensor_tensor(K0_, K1_, -2048.0, KK_, ALU.mult, ALU.add)
            v.tensor_copy(RR_, TH_)
            for c in PI_CH:
                v.scalar_tensor_tensor(T1_, K1_, -2048.0 * c, RR_, ALU.mult, ALU.add)
                v.scalar_tensor_tensor(RR_, K0_, -c, T1_, ALU.mult, ALU.add)
            v.tensor_scalar(T3_, KK_, 0.5, None, ALU.mult)
            rne(v, T2_, T3_, T1_, ANB)
            v.scalar_tensor_tensor(SG_, T2_, -2.0, KK_, ALU.mult, ALU.add)
            v.tensor_mul(T1_, SG_, SG_)
            v.tensor_scalar(SG_, T1_, -2.0, 1.0, ALU.mult, ALU.add)
            for _ in range(2):
                v.tensor_single_scalar(T1_, RR_, PI_HALF, ALU.is_gt)
                v.scalar_tensor_tensor(T2_, T1_, -PI_F32, RR_, ALU.mult, ALU.add)
                v.tensor_copy(RR_, T2_)
                v.tensor_scalar(T2_, T1_, -2.0, 1.0, ALU.mult, ALU.add)
                v.tensor_mul(T3_, SG_, T2_)
                v.tensor_copy(SG_, T3_)
                v.tensor_single_scalar(T1_, RR_, -PI_HALF, ALU.is_lt)
                v.scalar_tensor_tensor(T2_, T1_, PI_F32, RR_, ALU.mult, ALU.add)
                v.tensor_copy(RR_, T2_)
                v.tensor_scalar(T2_, T1_, -2.0, 1.0, ALU.mult, ALU.add)
                v.tensor_mul(T3_, SG_, T2_)
                v.tensor_copy(SG_, T3_)
            return v.tensor_copy(T3_, RR_)
        ch.step("vector", _tbl)

        def _sins(s):
            s.activation(T1_, RR_, ACT.Sin)
            s.activation(T2_, RR_, ACT.Abs)
            return s.activation(T3_, T2_, ACT.Sin, bias=cvec(PI_HALF), scale=cvec(-1.0))
        ch.step("scalar", _sins)

        def _tblf(v):
            v.tensor_mul(sin1[:], T1_, SG_)
            return v.tensor_mul(cos1[:], T3_, SG_)
        ch.step("vector", _tblf)

        if tap("cos1", [128, 512]) is not None:
            tap("sin1", [128, 512])

            def _tapc(sy):
                sy.dma_start(out=dbg["cos1"], in_=cos1[:]).then_inc(SY, 16)
                sy.dma_start(out=dbg["sin1"], in_=sin1[:]).then_inc(SY, 16)
            ch.step("sync", _tapc, dmas=2)

        def emit_ln_prep(li):
            MEANS = st16[:, 16:32]
            RSTD = st16[:, 32:48]

            def _ln1(v):
                dv = P1[:, 0:1024]
                sq = P1[:, 1024:2048]
                su = st16[:, 48:64]
                ssq = st16[:, 64:80]
                for ti in range(NT):
                    ins = v.reduce_sum(su[:, ti:ti + 1], x_res[:, ti * D:(ti + 1) * D], axis=AXL.X)
                vf(v, ins)
                ins = v.tensor_scalar(MEANS, su, float(1.0 / D), None, ALU.mult)
                vf(v, ins)
                for ti in range(NT):
                    v.tensor_sub(dv, x_res[:, ti * D:(ti + 1) * D],
                                 MEANS[:, ti:ti + 1].broadcast_to((128, D)))
                    v.tensor_mul(sq, dv, dv)
                    ins = v.reduce_sum(ssq[:, ti:ti + 1], sq, axis=AXL.X)
                vf(v, ins)
                return v.tensor_scalar(st8[:, 0:16], ssq, float(1.0 / D), 1e-5, ALU.mult, ALU.add)
            ch.step("vector", _ln1)
            ch.step("scalar", lambda s: s.activation(st8[:, 16:32], st8[:, 0:16], ACT.Sqrt))
            ch.step("vector", lambda v: v.reciprocal(RSTD, st8[:, 16:32]))

            xn1 = P1[:, 0:1024]
            qx1 = P1[:, 1024:2048]
            ta1 = P1[:, 2048:3072]
            tb1 = P1[:, 3072:4096]
            tc1 = P1[:, 4096:5120]
            xnT1 = P1[:, 5120:6144]
            qxT1 = P1[:, 6144:7168]
            for ti in range(NT):
                def _xt(v, ti=ti):
                    v.tensor_sub(ta1, x_res[:, ti * D:(ti + 1) * D],
                                 MEANS[:, ti:ti + 1].broadcast_to((128, D)))
                    v.tensor_mul(xn1, ta1, RSTD[:, ti:ti + 1].broadcast_to((128, D)))
                    if li == 0:
                        ins = v.reduce_max(st16[:, 80 + ti:81 + ti], xn1, axis=AXL.X, apply_absolute_value=True)
                        vf(v, ins)
                        vf(v, v.tensor_scalar(st8[:, 8:16], st16[:, 73 + ti:81 + ti], 1e-5, None, ALU.max))
                        vf(v, v.reciprocal(st8[:, 16:24], st8[:, 8:16]))
                        vf(v, v.tensor_scalar(st8[:, 24:32], st8[:, 16:24], 127.0, None, ALU.mult))
                        v.tensor_mul(ta1, xn1, st8[:, 31:32].broadcast_to((128, D)))
                        return rne(v, qx1, ta1, tb1, tc1)
                    else:
                        return fp4q(v, qx1, xn1, ta1, tb1, tc1,
                                    st16[:, 112:176], st16[:, 176:240], nblk=64)
                ch.step("vector", _xt)

                def _xtr(t, ti=ti):
                    for i in range(8):
                        t.transpose(pS[:, i * 128:(i + 1) * 128], xn1[:, i * 128:(i + 1) * 128], ident[:])
                    for i in range(8):
                        ins = t.transpose(pS[:, 1024 + i * 128:1152 + i * 128],
                                          qx1[:, i * 128:(i + 1) * 128], ident[:])
                    return ins
                ch.step("tensor", _xtr)

                def _xev(v, ti=ti):
                    v.tensor_copy(xnT1, pS[:, 0:1024])
                    if li == 0:
                        return v.tensor_copy(WB[:, 0:1024], pS[:, 1024:2048])
                    return v.tensor_copy(qxT1, pS[:, 1024:2048])
                ch.step("vector", _xev)

                def _xpk(sy, ti=ti):
                    sy.dma_start(out=xnT_park.rearrange("(a p) t -> p a t", p=128)[:, :, ti * 128:(ti + 1) * 128],
                                 in_=xnT1.rearrange("p (a c) -> p a c", c=128)).then_inc(SY, 16)
                    if li == 0:
                        sy.dma_start(out=qxT_park.rearrange("(a p) t -> p a t", p=128)[:, :, ti * 128:(ti + 1) * 128],
                                     in_=WB[:, 0:1024].rearrange("p (a c) -> p a c", c=128)).then_inc(SY, 16)
                    else:
                        sy.dma_start(out=xqT_park.rearrange("(a p) t -> p a t", p=128)[:, :, ti * 128:(ti + 1) * 128],
                                     in_=qxT1.rearrange("p (a c) -> p a c", c=128)).then_inc(SY, 16)
                ch.step("sync", _xpk, dmas=2)

        def emit_l0_actscales():
            def _sx(v):
                vf(v, v.tensor_scalar(st8[:, 0:16], st16[:, 80:96], 1e-5, None, ALU.max))
                vf(v, v.reciprocal(st8[:, 16:32], st8[:, 0:16]))
                v.tensor_scalar(st16[:, 96:112], st8[:, 0:16], float(1.0 / 127.0), None, ALU.mult)
                return v.tensor_scalar(st16[:, 80:96], st8[:, 16:32], 127.0, None, ALU.mult)
            ch.step("vector", _sx)

        def emit_ternary_wqkv():
            wn = P1[:, 0:1024]
            wf = P1[:, 1024:2048]
            wq_ = P1[:, 2048:3072]
            ta_ = P1[:, 3072:4096]
            tb_ = P1[:, 4096:5120]
            wb16 = WB[:, 0:1024]
            gs16 = WB[:, 1024:1040]
            for ci in range(12):
                def _wld(sy, ci=ci):
                    sy.dma_start(out=wn, in_=w_qkv[ci * 128:(ci + 1) * 128, :]).then_inc(SY, 16)
                ch.step("sync", _wld, dmas=1)

                def _wq(v, ci=ci):
                    gsA = st16[:, 240:256]
                    gsB = st16[:, 224:240]
                    v.tensor_copy(wb16, wn)
                    v.tensor_copy(wf, wb16)
                    v.tensor_scalar(wq_, wf, -1.0, None, ALU.mult)
                    v.tensor_max(ta_, wf, wq_)
                    ins = v.reduce_sum(gsA.rearrange("p (g o) -> p g o", o=1), ta_.rearrange("p (g i) -> p g i", i=64), axis=AXL.X)
                    vf(v, ins)
                    vf(v, v.tensor_scalar(gsB, gsA, float(1.0 / 64.0), None, ALU.mult))
                    v.tensor_scalar(gs16, gsB, 1.0, None, ALU.mult)
                    v.tensor_scalar(gsA, gs16, 1.0, None, ALU.mult)
                    vf(v, v.tensor_scalar(gsB, gsA, BF16_1EM8, None, ALU.max))
                    vf(v, v.reciprocal(gsA, gsB))
                    mulb(v, ta_, wf, gsA, 16, 64)
                    v.tensor_copy(wb16, ta_)
                    v.tensor_copy(ta_, wb16)
                    rne(v, wq_, ta_, tb_, wn)
                    ins = v.tensor_scalar(ta_, wq_, 1.0, -1.0, ALU.min, ALU.max)
                    vf(v, ins)
                    mulb(v, wq_, ta_, gsB, 16, 64)
                    return v.tensor_copy(wf.bitcast(I32), wq_.bitcast(I32))
                ch.step("vector", _wq)

                def _wtr(t, ci=ci):
                    for i in range(8):
                        ins = t.transpose(pS[:, i * 128:(i + 1) * 128], wf[:, i * 128:(i + 1) * 128], ident[:])
                    return ins
                ch.step("tensor", _wtr)

                def _wev(v, ci=ci):
                    return v.tensor_copy(WP[:].rearrange("p (a c) -> p a c", c=1536)[:, :, ci * 128:(ci + 1) * 128],
                                         pS[:, 0:1024].rearrange("p (a c) -> p a c", c=128))
                ch.step("vector", _wev)

        def emit_ternary_wproj():
            wn = P1[:, 0:1024]
            wf = P1[:, 1024:2048]
            wq_ = P1[:, 2048:3072]
            ta_ = P1[:, 3072:4096]
            tb_ = P1[:, 4096:5120]
            wb16 = WB[:, 0:1024]
            gs16 = WB[:, 1024:1040]
            for ci in range(8):
                def _wld(sy, ci=ci):
                    sy.dma_start(out=wn, in_=w_proj[ci * 128:(ci + 1) * 128, :]).then_inc(SY, 16)
                ch.step("sync", _wld, dmas=1)

                def _wq(v, ci=ci):
                    gsA = st16[:, 240:256]
                    gsB = st16[:, 224:240]
                    v.tensor_copy(wb16, wn)
                    v.tensor_copy(wf, wb16)
                    v.tensor_scalar(wq_, wf, -1.0, None, ALU.mult)
                    v.tensor_max(ta_, wf, wq_)
                    ins = v.reduce_sum(gsA.rearrange("p (g o) -> p g o", o=1), ta_.rearrange("p (g i) -> p g i", i=64), axis=AXL.X)
                    vf(v, ins)
                    vf(v, v.tensor_scalar(gsB, gsA, float(1.0 / 64.0), None, ALU.mult))
                    v.tensor_scalar(gs16, gsB, 1.0, None, ALU.mult)
                    v.tensor_scalar(gsA, gs16, 1.0, None, ALU.mult)
                    vf(v, v.tensor_scalar(gsB, gsA, BF16_1EM8, None, ALU.max))
                    vf(v, v.reciprocal(gsA, gsB))
                    mulb(v, ta_, wf, gsA, 16, 64)
                    v.tensor_copy(wb16, ta_)
                    v.tensor_copy(ta_, wb16)
                    rne(v, wq_, ta_, tb_, wn)
                    ins = v.tensor_scalar(ta_, wq_, 1.0, -1.0, ALU.min, ALU.max)
                    vf(v, ins)
                    mulb(v, wq_, ta_, gsB, 16, 64)
                    return v.tensor_copy(wf.bitcast(I32), wq_.bitcast(I32))
                ch.step("vector", _wq)

                def _wtr(t, ci=ci):
                    for i in range(8):
                        ins = t.transpose(pS[:, i * 128:(i + 1) * 128], wf[:, i * 128:(i + 1) * 128], ident[:])
                    return ins
                ch.step("tensor", _wtr)

                def _wev(v, ci=ci):
                    return v.tensor_copy(WP[:, 0:8192].rearrange("p (a c) -> p a c", c=1024)[:, :, ci * 128:(ci + 1) * 128],
                                         pS[:, 0:1024].rearrange("p (a c) -> p a c", c=128))
                ch.step("vector", _wev)

        def emit_qkv_l0():
            xnTt = P1[:, 0:1024]
            qn_ = P1[:, 1024:2048]
            kn_ = P1[:, 2048:3072]
            qr_ = P1[:, 3072:4096]
            kr_ = P1[:, 4096:5120]
            sqA = P1[:, 5120:6144]
            sqB = P1[:, 6144:7168]
            vb_ = P1[:, 7168:8208]
            for ti in range(NT):
                def _qld(sy, ti=ti):
                    sy.dma_start(out=xnTt.rearrange("p (a c) -> p a c", c=128),
                                 in_=xnT_park.rearrange("(a p) t -> p a t", p=128)[:, :, ti * 128:(ti + 1) * 128]).then_inc(SY, 16)
                ch.step("sync", _qld, dmas=1)

                def _qmm(t, ti=ti):
                    for kd in range(8):
                        for n0 in range(0, 1536, 512):
                            ins = t.matmul(pS[:, n0:n0 + 512], xnTt[:, kd * 128:(kd + 1) * 128],
                                           WP[:, kd * 1536 + n0:kd * 1536 + n0 + 512],
                                           start=(kd == 0), stop=(kd == 7))
                    return ins
                ch.step("tensor", _qmm)

                def _qsq(s, ti=ti):
                    return s.activation(P1[:, 3072:4352], pS[:, 0:1280], ACT.Square)
                ch.step("scalar", _qsq)

                def _qrms(v, ti=ti):
                    v.reduce_sum(st16[:, 240:256].rearrange("p (g o) -> p g o", o=1),
                                 P1[:, 3072:4096].rearrange("p (g i) -> p g i", i=64), axis=AXL.X)
                    ins = v.reduce_sum(st16[:, 236:240].rearrange("p (g o) -> p g o", o=1),
                                 P1[:, 4096:4352].rearrange("p (g i) -> p g i", i=64), axis=AXL.X)
                    vf(v, ins)
                    vf(v, v.tensor_scalar(st16[:, 216:236], st16[:, 236:256], float(1.0 / 64.0), None, ALU.mult))
                    return v.tensor_add(st16[:, 196:216], st16[:, 216:236], epsc[:, 0:20])
                ch.step("vector", _qrms)
                ch.step("scalar", lambda s: s.activation(st16[:, 216:236], st16[:, 196:216], ACT.Sqrt))

                def _qnrm(v, ti=ti):
                    rs = st16[:, 196:216]
                    ins = v.reciprocal(rs, st16[:, 216:236])
                    vf(v, ins)
                    mulb(v, kn_[:, 0:256], pS[:, 1024:1280], rs[:, 0:4], 4, 64)
                    return mulb(v, qn_, pS[:, 0:1024], rs[:, 4:20], 16, 64)
                ch.step("vector", _qnrm)

                def _rope(v, ti=ti):
                    cs = cos0[:, ti * 32:(ti + 1) * 32]
                    sn = sin0[:, ti * 32:(ti + 1) * 32]
                    for (src, dst, nh) in ((qn_, qr_, 16), (kn_, kr_, 4)):
                        s4 = src.rearrange("p (h z c) -> p h z c", z=2, c=32)
                        d4 = dst.rearrange("p (h z c) -> p h z c", z=2, c=32)
                        csb = cs.unsqueeze(1).broadcast_to((128, nh, 32))
                        snb = sn.unsqueeze(1).broadcast_to((128, nh, 32))
                        a = sqA[:, 0:nh * 32].rearrange("p (h c) -> p h c", c=32)
                        b = sqB[:, 0:nh * 32].rearrange("p (h c) -> p h c", c=32)
                        v.tensor_mul(a, s4[:, 0:nh, 0, :], csb)
                        v.tensor_mul(b, s4[:, 0:nh, 1, :], snb)
                        v.tensor_add(d4[:, 0:nh, 0, :], a, b)
                        v.tensor_mul(a, s4[:, 0:nh, 1, :], csb)
                        v.tensor_mul(b, s4[:, 0:nh, 0, :], snb)
                        v.tensor_sub(d4[:, 0:nh, 1, :], a, b)
                    v.memset(vb_, 0.0)
                    v.tensor_copy(vb_.rearrange("p (h c) -> p h c", c=65)[:, 0:4, 0:64],
                                  pS[:, 1280:1536].rearrange("p (h c) -> p h c", c=64))
                    return v.memset(vb_.rearrange("p (h c) -> p h c", c=65)[:, 0:4, 64:65], 1.0)
                ch.step("vector", _rope)

                def _qtr(t, ti=ti):
                    for i in range(8):
                        t.transpose(pS[:, i * 128:(i + 1) * 128], qr_[:, i * 128:(i + 1) * 128], ident[:])
                    for i in range(2):
                        ins = t.transpose(pS[:, 1024 + i * 128:1152 + i * 128],
                                          kr_[:, i * 128:(i + 1) * 128], ident[:])
                    return ins
                ch.step("tensor", _qtr)

                def _qev(v, ti=ti):
                    v.tensor_copy(qr_, pS[:, 0:1024])
                    return v.tensor_copy(kr_[:, 0:256], pS[:, 1024:1280])
                ch.step("vector", _qev)

                def _qpk(sy, ti=ti):
                    sy.dma_start(out=qT_park.rearrange("(a p) t -> p a t", p=128)[:, :, ti * 128:(ti + 1) * 128],
                                 in_=qr_.rearrange("p (a c) -> p a c", c=128)).then_inc(SY, 16)
                    sy.dma_start(out=kT_park.rearrange("(a p) t -> p a t", p=128)[:, 0:2, ti * 128:(ti + 1) * 128],
                                 in_=kr_[:, 0:256].rearrange("p (a c) -> p a c", c=128)).then_inc(SY, 16)
                    sy.dma_start(out=v_park[ti * 128:(ti + 1) * 128, :], in_=vb_).then_inc(SY, 16)
                ch.step("sync", _qpk, dmas=3)

        def emit_qkv_l1():
            xnTt = P1[:, 0:1024]
            pr_ = P1[:, 1024:2048]
            rr2 = P1[:, 2048:3072]
            sqA = P1[:, 3072:4096]
            sqB = P1[:, 4096:5120]
            vb_ = P1[:, 7168:8208]
            for mi, wd in enumerate((wq1T_d, wk1T_d, wv1T_d)):
                def _wl(sy, wd=wd):
                    sy.dma_start(out=WP[:, 0:8192].rearrange("p (a c) -> p a c", c=1024),
                                 in_=wd.rearrange("(a p) c -> p a c", p=128)).then_inc(SY, 16)
                ch.step("sync", _wl, dmas=1)
                for ti in range(NT):
                    def _qld(sy, ti=ti):
                        sy.dma_start(out=xnTt.rearrange("p (a c) -> p a c", c=128),
                                     in_=xnT_park.rearrange("(a p) t -> p a t", p=128)[:, :, ti * 128:(ti + 1) * 128]).then_inc(SY, 16)
                    ch.step("sync", _qld, dmas=1)

                    def _mm(t, ti=ti):
                        for kd in range(8):
                            for n0 in (0, 512):
                                ins = t.matmul(pS[:, n0:n0 + 512], xnTt[:, kd * 128:(kd + 1) * 128],
                                               WP[:, kd * 1024 + n0:kd * 1024 + n0 + 512],
                                               start=(kd == 0), stop=(kd == 7))
                        return ins
                    ch.step("tensor", _mm)

                    if mi < 2:
                        def _rope1(v, ti=ti):
                            cs = cos1[:, ti * 32:(ti + 1) * 32]
                            sn = sin1[:, ti * 32:(ti + 1) * 32]
                            s4 = pS[:, 0:1024].rearrange("p (h c z) -> p h c z", c=32, z=2)
                            d4 = pr_.rearrange("p (h c z) -> p h c z", c=32, z=2)
                            csb = cs.unsqueeze(1).broadcast_to((128, 16, 32))
                            snb = sn.unsqueeze(1).broadcast_to((128, 16, 32))
                            a = sqA.rearrange("p (h c) -> p h c", c=64)[:, :, 0:32]
                            b = sqB.rearrange("p (h c) -> p h c", c=64)[:, :, 0:32]
                            v.tensor_mul(a, s4[:, :, :, 0], csb)
                            v.tensor_mul(b, s4[:, :, :, 1], snb)
                            v.tensor_sub(d4[:, :, :, 0], a, b)
                            v.tensor_mul(a, s4[:, :, :, 0], snb)
                            v.tensor_mul(b, s4[:, :, :, 1], csb)
                            return v.tensor_add(d4[:, :, :, 1], a, b)
                        ch.step("vector", _rope1)

                        def _tr1(t, ti=ti):
                            for i in range(8):
                                ins = t.transpose(pS[:, 1024 + i * 128:1152 + i * 128],
                                                  pr_[:, i * 128:(i + 1) * 128], ident[:])
                            return ins
                        ch.step("tensor", _tr1)
                        ch.step("vector", lambda v: v.tensor_copy(rr2, pS[:, 1024:2048]))

                        def _pk1(sy, ti=ti, mi=mi):
                            park = qT_park if mi == 0 else kT_park
                            sy.dma_start(out=park.rearrange("(a p) t -> p a t", p=128)[:, :, ti * 128:(ti + 1) * 128],
                                         in_=rr2.rearrange("p (a c) -> p a c", c=128)).then_inc(SY, 16)
                        ch.step("sync", _pk1, dmas=1)
                    else:
                        def _vaug(v, ti=ti):
                            v.memset(vb_, 0.0)
                            v.tensor_copy(vb_.rearrange("p (h c) -> p h c", c=65)[:, :, 0:64],
                                          pS[:, 0:1024].rearrange("p (h c) -> p h c", c=64))
                            return v.memset(vb_.rearrange("p (h c) -> p h c", c=65)[:, :, 64:65], 1.0)
                        ch.step("vector", _vaug)

                        def _pkv(sy, ti=ti):
                            sy.dma_start(out=v_park[ti * 128:(ti + 1) * 128, :], in_=vb_).then_inc(SY, 16)
                        ch.step("sync", _pkv, dmas=1)

        def emit_attn(li):
            qTg = P1[:, 0:256]
            kTh = P1[:, 256:2304]
            vbh = P1[:, 2304:3344]
            ebuf = P1[:, 3344:5392]
            ySb = P1[:, 5392:7440]
            ynb = P1[:, 3344:5392]
            sqb = P2[:, 0:2048]
            msk = st16[:, 128:256][:, 0:128]
            for g in range(8):
                gmax = (2 * g + 2) * 128 if li == 0 else T
                nkt = gmax // 128
                for h in range(16):
                    kvh = (h // 4) if li == 0 else h

                    def _ald(sy, g=g, h=h, kvh=kvh, gmax=gmax):
                        sy.dma_start(out=qTg[0:64, :],
                                     in_=qT_park[h * 64:(h + 1) * 64, g * 256:(g + 1) * 256]).then_inc(SY, 16)
                        sy.dma_start(out=kTh[0:64, 0:gmax],
                                     in_=kT_park[kvh * 64:kvh * 64 + 64, 0:gmax]).then_inc(SY, 16)
                        sy.dma_start(out=vbh[:, 0:(gmax // 128) * 65].rearrange("p (a c) -> p a c", c=65),
                                     in_=v_park.rearrange("(a p) c -> p a c", p=128)[:, 0:gmax // 128, kvh * 65:(kvh + 1) * 65]).then_inc(SY, 16)
                    ch.step("sync", _ald, dmas=3)

                    for qi in range(2):
                        qg = 2 * g + qi
                        ext = (qg + 1) * 128 if li == 0 else T

                        def _sc(t, qi=qi, ext=ext):
                            for n0 in range(0, ext, 512):
                                n1 = min(n0 + 512, ext)
                                ins = t.matmul(pS[:, n0:n1], qTg[0:64, qi * 128:(qi + 1) * 128],
                                               kTh[0:64, n0:n1], start=True, stop=True)
                            return ins
                        ch.step("tensor", _sc)

                        sclAP = gain16[:, h:h + 1] if li == 0 else cvec(0.125)

                        def _mx(v, h=h, ext=ext, sclAP=sclAP):
                            ins = v.reduce_max(st8[:, 0:1], pS[:, 0:ext], axis=AXL.X, negate=True)
                            vf(v, ins)
                            return v.tensor_mul(st8[:, 8:16], st8[:, 0:8],
                                                sclAP.broadcast_to((128, 8)))
                        ch.step("vector", _mx)

                        def _ex(s, h=h, ext=ext, sclAP=sclAP):
                            return s.activation(ebuf[:, 0:ext], pS[:, 0:ext], ACT.Exp,
                                                scale=sclAP, bias=st8[:, 8:9])
                        ch.step("scalar", _ex)

                        if li == 0:
                            def _msk(v, ext=ext):
                                v.tensor_mul(msk, ebuf[:, ext - 128:ext], tril[:])
                                return v.tensor_scalar(ebuf[:, ext - 128:ext], msk, 1.0, None, ALU.mult)
                            ch.step("vector", _msk)

                        def _etr(t, ext=ext):
                            for kt in range(ext // 128):
                                ins = t.transpose(pS[:, kt * 128:(kt + 1) * 128],
                                                  ebuf[:, kt * 128:(kt + 1) * 128], ident[:])
                            return ins
                        ch.step("tensor", _etr)

                        if qi == 0:
                            def _eev(v, qi=qi, ext=ext):
                                kte = ext // 128
                                return v.tensor_copy(
                                    P2[:, 0:kte * 256].rearrange("p (k c) -> p k c", c=256)[:, :, qi * 128:(qi + 1) * 128],
                                    pS[:, 0:ext].rearrange("p (k c) -> p k c", c=128))
                            ch.step("vector", _eev)
                        else:
                            eev_toks = []
                            for kt in range(ext // 128):
                                def _eev1(v, kt=kt):
                                    return v.tensor_copy(
                                        P2[:, kt * 256 + 128:kt * 256 + 256],
                                        pS[:, kt * 128:(kt + 1) * 128])
                                eev_toks.append(ch.step("vector", _eev1))

                    first = True
                    for kt in range(nkt):
                        sq0 = max(0, kt - 2 * g) if li == 0 else 0
                        if sq0 >= 2:
                            continue

                        def _av(t, g=g, kt=kt, sq0=sq0, first=first, nkt=nkt):
                            return t.matmul(pB[0:65, sq0 * 128:256],
                                            vbh[:, kt * 65:(kt + 1) * 65],
                                            P2[:, kt * 256 + sq0 * 128:kt * 256 + 256],
                                            start=first, stop=(kt == nkt - 1))
                        ch.step("tensor", _av, after=[eev_toks[kt]])
                        first = False

                    ch.step("vector", lambda v: v.reciprocal(rowB[0:1, 0:256], pB[64:65, 0:256]))
                    ch.step("tensor", lambda t: t.matmul(pC[0:64, 0:256], ones_row[0:1, 0:64],
                                                         rowB[0:1, 0:256], start=True, stop=True))
                    ch.step("vector", lambda v: v.tensor_copy(P1[0:64, 7440:7696], pC[0:64, 0:256]))

                    def _ysc(v, h=h):
                        return v.tensor_mul(ySb[(h % 2) * 64:(h % 2) * 64 + 64, (h // 2) * 256:(h // 2) * 256 + 256],
                                            pB[0:64, 0:256], P1[0:64, 7440:7696])
                    ch.step("vector", _ysc)

                if li == 0:
                    ch.step("scalar", lambda s: s.activation(sqb, ySb, ACT.Square))

                    def _ss(t):
                        for dt_ in range(8):
                            ins = t.matmul(pC[0:1, 0:256], ones_col[:, 0:1], sqb[:, dt_ * 256:(dt_ + 1) * 256],
                                           start=(dt_ == 0), stop=(dt_ == 7))
                        return ins
                    ch.step("tensor", _ss)

                    def _rmrow(v):
                        vf(v, v.tensor_scalar(rowB[0:1, 0:256], pC[0:1, 0:256], float(1.0 / D), None, ALU.mult))
                        return v.tensor_add(rowC[0:1, 0:256], rowB[0:1, 0:256],
                                            epsc[0:1, 0:1].broadcast_to((1, 256)))
                    ch.step("vector", _rmrow)
                    ch.step("scalar", lambda s: s.activation(rowC[0:1, 256:512], rowC[0:1, 0:256], ACT.Sqrt))
                    ch.step("vector", lambda v: v.reciprocal(rowC[0:1, 512:768], rowC[0:1, 256:512]))
                    ch.step("tensor", lambda t: t.matmul(pC[:, 0:256], ones_row[:], rowC[0:1, 512:768],
                                                         start=True, stop=True))

                    def _ynm(v):
                        return v.tensor_mul(ynb.rearrange("p (a c) -> p a c", c=256),
                                            ySb.rearrange("p (a c) -> p a c", c=256),
                                            pC[:, 0:256].unsqueeze(1).broadcast_to((128, 8, 256)))
                    ch.step("vector", _ynm)
                    ysrc = ynb
                else:
                    ysrc = ySb

                def _proj(t, ysrc=ysrc):
                    for dmo in range(8):
                        for dmy in range(8):
                            ins = t.matmul(pS[:, dmo * 256:(dmo + 1) * 256],
                                           WP[:, dmy * 1024 + dmo * 128:dmy * 1024 + (dmo + 1) * 128],
                                           ysrc[:, dmy * 256:(dmy + 1) * 256],
                                           start=(dmy == 0), stop=(dmy == 7))
                    return ins
                ch.step("tensor", _proj)
                ch.step("vector", lambda v: v.tensor_copy(sqb, pS[:, 0:2048]))

                for qi in range(2):
                    def _otr(t, qi=qi):
                        for dmo in range(8):
                            ins = t.transpose(pA[:, dmo * 128:(dmo + 1) * 128],
                                              sqb[:, dmo * 256 + qi * 128:dmo * 256 + (qi + 1) * 128], ident[:])
                        return ins
                    ch.step("tensor", _otr)

                    def _oad(v, g=g, qi=qi):
                        return resadd(v, 2 * g + qi, pA[:, 0:1024], 1024, P1[:, 7440:8464])
                    ch.step("vector", _oad)

        def emit_ffn_l0():
            acc0 = st16[:, 240:242]

            def _wsum_init(v):
                v.memset(st8[:], 0.0)
                return v.memset(st16[:, 224:256], 0.0)
            ch.step("vector", _wsum_init)
            for wi, wd in enumerate((w1T_d, w2T_d)):
                for c in range(8):
                    if wi == 0:
                        def _ws(sy, wd=wd, c=c):
                            sy.dma_start(out=P1[:, 0:4096].rearrange("p (a c) -> p a c", c=512),
                                         in_=wd.rearrange("(a p) c -> p a c", p=128)[:, :, c * 512:(c + 1) * 512]).then_inc(SY, 16)
                    else:
                        def _ws(sy, wd=wd, c=c):
                            sy.dma_start(out=P1[:, 0:4096].rearrange("p (a c) -> p a c", c=1024),
                                         in_=wd.rearrange("(a p) c -> p a c", p=128)[:, c * 4:(c + 1) * 4, :]).then_inc(SY, 16)
                    ch.step("sync", _ws, dmas=1)

                    def _wsr(v, wi=wi, c=c):
                        acc = st16[:, 224 + 8 * wi:232 + 8 * wi]
                        v.tensor_scalar(P2[:, 0:4096], P1[:, 0:4096], -1.0, None, ALU.mult)
                        v.tensor_max(P1[:, 4096:8192], P1[:, 0:4096], P2[:, 0:4096])
                        ins = v.reduce_sum(st8[:, 7:8], P1[:, 4096:8192], axis=AXL.X)
                        vf(v, ins)
                        vf(v, v.tensor_add(st8[:, 8:16], st8[:, 0:8], acc))
                        return v.tensor_scalar(acc, st8[:, 8:16], 1.0, None, ALU.mult)
                    ch.step("vector", _wsr)

            def _wsfin(t):
                return t.matmul(pC[0:1, 0:16], ones_col[:, 0:1], st16[:, 224:240], start=True, stop=True)
            ch.step("tensor", _wsfin)

            def _wsca(v):
                vf(v, v.tensor_scalar(rowB[0:1, 0:16], pC[0:1, 0:16], float(1.0 / (FI * D)), None, ALU.mult))
                vf(v, v.tensor_scalar(rowB[0:1, 16:32], rowB[0:1, 0:16], 1e-5, None, ALU.max))
                return v.reciprocal(rowB[0:1, 32:48], rowB[0:1, 16:32])
            ch.step("vector", _wsca)
            ch.step("tensor", lambda t: t.matmul(pC[:, 0:48], ones_row[:], rowB[0:1, 0:48], start=True, stop=True))
            ch.step("vector", lambda v: v.tensor_scalar(st16[:, 0:48], pC[:, 0:48], 1.0, None, ALU.mult))
            tst = tap("st16b", [128, 256])
            if tst is not None:
                def _tst(sy):
                    sy.dma_start(out=tst, in_=st16[:]).then_inc(SY, 16)
                ch.step("sync", _tst, dmas=1)

            def _rx1(sy):
                sy.dma_start(out=row_bnc, in_=st16[:, 96:112]).then_inc(SY, 16)
            ch.step("sync", _rx1, dmas=1)

            def _rx2(sy):
                sy.dma_start(out=rowA[0:1, :], in_=row_bnc.transpose([1, 0]),
                             allow_slow_non_contiguous=True).then_inc(SY, 16)
            ch.step("sync", _rx2, dmas=1)

            amax = P2[:, 2048:4096]

            def _am0(v):
                return v.memset(amax, 0.0)
            ch.step("vector", _am0)

            for c in range(8):
                def _w1ld(sy, c=c):
                    sy.dma_start(out=P1[:, 0:4096].rearrange("p (a c) -> p a c", c=512),
                                 in_=w1T_d.rearrange("(a p) c -> p a c", p=128)[:, :, c * 512:(c + 1) * 512]).then_inc(SY, 16)
                    sy.dma_start(out=st16[:, 224:236].rearrange("p (a c) -> p a c", c=3),
                                 in_=convw_d.rearrange("(a p) c -> p a c", p=128)[:, c * 4:(c + 1) * 4, :]).then_inc(SY, 16)
                    sy.dma_start(out=st16[:, 236:240].rearrange("p (a c) -> p a c", c=1),
                                 in_=convb_d.rearrange("(a p) c -> p a c", p=128)[:, c * 4:(c + 1) * 4, :],
                                 allow_slow_non_contiguous=True).then_inc(SY, 16)
                ch.step("sync", _w1ld, dmas=3)
                if c == 0:
                    twl = tap("w1ld", [128, 4096])
                    if twl is not None:
                        def _twl(sy):
                            sy.dma_start(out=twl, in_=P1[:, 0:4096]).then_inc(SY, 16)
                        ch.step("sync", _twl, dmas=1)

                def _w1qa(v, c=c):
                    v.tensor_scalar(st8[:, 32:40], st16[:, 32:40], 1.0, None, ALU.mult)
                    v.tensor_mul(P1[:, 4096:6144], P1[:, 0:2048], st8[:, 39:40].broadcast_to((128, 2048)))
                    return v.tensor_mul(P1[:, 6144:8192], P1[:, 2048:4096], st8[:, 39:40].broadcast_to((128, 2048)))
                ch.step("vector", _w1qa)
                if c == 0:
                    tmu = tap("w1mul", [128, 4096])
                    if tmu is not None:
                        def _tmu(sy):
                            sy.dma_start(out=tmu, in_=P1[:, 4096:8192]).then_inc(SY, 16)
                        ch.step("sync", _tmu, dmas=1)
                    ts8 = tap("st8s", [128, 40])
                    if ts8 is not None:
                        def _ts8(sy):
                            sy.dma_start(out=ts8, in_=st8[:]).then_inc(SY, 16)
                        ch.step("sync", _ts8, dmas=1)

                def _w1q(v, c=c):
                    for hh in range(4):
                        rne(v, P1[:, hh * 1024:(hh + 1) * 1024], P1[:, 4096 + hh * 1024:4096 + (hh + 1) * 1024],
                            P2[:, 0:1024], P2[:, 1024:2048])
                    v.tensor_scalar(P1[:, 4096:8192], P1[:, 0:4096], 1.0, -1.0, ALU.min, ALU.max)
                    return v.tensor_copy(WB[:, 0:4096], P1[:, 4096:8192])
                ch.step("vector", _w1q)

                if c == 0:
                    twr = tap("w1raw", [128, 4096])
                    if twr is not None:
                        def _twr(sy):
                            sy.dma_start(out=twr, in_=P1[:, 0:4096]).then_inc(SY, 16)
                        ch.step("sync", _twr, dmas=1)
                    tw1 = tap("w1qs", [128, 4096], BF16)
                    if tw1 is not None:
                        def _tw1(sy):
                            sy.dma_start(out=tw1, in_=WB[:, 0:4096]).then_inc(SY, 16)
                        ch.step("sync", _tw1, dmas=1)
                for tc_ in range(4):
                    def _qxld(sy, tc_=tc_):
                        sy.dma_start(out=WB[:, 4096:8192].rearrange("p (a c) -> p a c", c=512),
                                     in_=qxT_park.rearrange("(a p) t -> p a t", p=128)[:, :, tc_ * 512:(tc_ + 1) * 512]).then_inc(SY, 16)
                    ch.step("sync", _qxld, dmas=1)

                    for fi in range(4):
                        def _w1mm(t, fi=fi, tc_=tc_):
                            for kd in range(8):
                                ins = t.matmul(pB[:, 0:512],
                                               WB[:, fi * 128 + 0:0] if False else WB[:, 0:4096].rearrange("p (a c) -> p a c", c=512)[:, kd, fi * 128:(fi + 1) * 128],
                                               WB[:, 4096:8192].rearrange("p (a c) -> p a c", c=512)[:, kd, :],
                                               start=(kd == 0), stop=(kd == 7))
                            return ins
                        ch.step("tensor", _w1mm)

                        def _hev(v, fi=fi, tc_=tc_, c=c):
                            return v.tensor_copy(WP[:, fi * 2050 + 1 + tc_ * 512:fi * 2050 + 1 + tc_ * 512 + 512], pB[:, 0:512])
                        ch.step("vector", _hev)

                for fi in range(4):
                    if fi == 0:
                        ch.step("tensor", lambda t: t.matmul(pS[:, 0:512], ones_row[:], rowA[0:1, 0:512], start=True, stop=True))
                        ch.step("tensor", lambda t: t.matmul(pS[:, 512:1024], ones_row[:], rowA[0:1, 512:1024], start=True, stop=True))
                        ch.step("tensor", lambda t: t.matmul(pS[:, 1024:1536], ones_row[:], rowA[0:1, 1024:1536], start=True, stop=True))
                        ch.step("tensor", lambda t: t.matmul(pS[:, 1536:2048], ones_row[:], rowA[0:1, 1536:2048], start=True, stop=True))

                    def _hsc(v, fi=fi):
                        hs = WP[:, fi * 2050:(fi + 1) * 2050]
                        v.memset(hs[:, 0:1], 0.0)
                        vf(v, v.memset(hs[:, 2049:2050], 0.0))
                        vf(v, v.tensor_copy(st8[:, 32:40], st16[:, 16:24]))
                        v.tensor_mul(P1[:, 0:2048], hs[:, 1:2049], pS[:, 0:2048])
                        return v.tensor_mul(hs[:, 1:2049], P1[:, 0:2048], st8[:, 39:40].broadcast_to((128, 2048)))
                    ch.step("vector", _hsc)

                    def _cnv(v, fi=fi):
                        hs = WP[:, fi * 2050:(fi + 1) * 2050]
                        w0 = st16[:, 224 + fi * 3:225 + fi * 3]
                        w1c = st16[:, 225 + fi * 3:226 + fi * 3]
                        w2c = st16[:, 226 + fi * 3:227 + fi * 3]
                        v.tensor_mul(P1[:, 0:2048], hs[:, 0:2048], w0.broadcast_to((128, 2048)))
                        v.tensor_mul(P1[:, 2048:4096], hs[:, 1:2049], w1c.broadcast_to((128, 2048)))
                        v.tensor_add(P1[:, 4096:6144], P1[:, 0:2048], P1[:, 2048:4096])
                        v.tensor_mul(P1[:, 0:2048], hs[:, 2:2050], w2c.broadcast_to((128, 2048)))
                        return v.tensor_add(P1[:, 2048:4096], P1[:, 4096:6144], P1[:, 0:2048])
                    ch.step("vector", _cnv)

                    def _silu(s, fi=fi):
                        return s.activation(P1[:, 4096:6144], P1[:, 2048:4096], ACT.Silu,
                                            bias=st16[:, 236 + fi:237 + fi])
                    ch.step("scalar", _silu)

                    def _amx(v, fi=fi):
                        v.tensor_scalar(P1[:, 0:2048], P1[:, 4096:6144], -1.0, None, ALU.mult)
                        v.tensor_max(P1[:, 2048:4096], P1[:, 4096:6144], P1[:, 0:2048])
                        v.tensor_max(P1[:, 0:2048], amax, P1[:, 2048:4096])
                        return v.tensor_copy(amax, P1[:, 0:2048])
                    ch.step("vector", _amx)

                    def _hpk(sy, c=c, fi=fi):
                        sy.dma_start(out=h_park[c * 512 + fi * 128:c * 512 + (fi + 1) * 128, :],
                                     in_=P1[:, 4096:6144]).then_inc(SY, 16)
                    ch.step("sync", _hpk, dmas=1)
                if c == 0:
                    thw = tap("hpre", [128, 2050])
                    if thw is not None:
                        def _thw(sy):
                            sy.dma_start(out=thw, in_=WP[:, 0:2050]).then_inc(SY, 16)
                        ch.step("sync", _thw, dmas=1)

            def _amtr(t):
                for i in range(16):
                    ins = t.transpose(pS[:, i * 128:(i + 1) * 128], amax[:, i * 128:(i + 1) * 128], ident[:])
                return ins
            ch.step("tensor", _amtr)

            def _amr(v):
                ins = v.reduce_max(st16[:, 224:240].rearrange("p (g o) -> p g o", o=1),
                             pS[:, 0:2048].rearrange("p (a c) -> p a c", c=128), axis=AXL.X)
                vf(v, ins)
                vf(v, v.tensor_scalar(st16[:, 240:256], st16[:, 224:240], 1e-5, None, ALU.max))
                vf(v, v.reciprocal(st16[:, 224:240], st16[:, 240:256]))
                v.tensor_scalar(st16[:, 208:224], st16[:, 224:240], 127.0, None, ALU.mult)
                ins = v.tensor_scalar(st16[:, 192:208], st16[:, 240:256], float(1.0 / 127.0), None, ALU.mult)
                vf(v, ins)
                v.tensor_scalar(st8[:, 32:40], st16[:, 24:32], 1.0, None, ALU.mult)
                return v.tensor_mul(st16[:, 176:192], st16[:, 192:208], st8[:, 39:40].broadcast_to((128, 16)))
            ch.step("vector", _amr)

            def _sh1(sy):
                sy.dma_start(out=row_bnc, in_=st16[:, 208:224]).then_inc(SY, 16)
            ch.step("sync", _sh1, dmas=1)

            def _sh2(sy):
                sy.dma_start(out=rowA[0:1, :], in_=row_bnc.transpose([1, 0]),
                             allow_slow_non_contiguous=True).then_inc(SY, 16)
            ch.step("sync", _sh2, dmas=1)

            def _sh3(sy):
                sy.dma_start(out=row_bnc, in_=st16[:, 176:192]).then_inc(SY, 16)
            ch.step("sync", _sh3, dmas=1)

            def _sh4(sy):
                sy.dma_start(out=rowC[0:1, :], in_=row_bnc.transpose([1, 0]),
                             allow_slow_non_contiguous=True).then_inc(SY, 16)
            ch.step("sync", _sh4, dmas=1)

            for c in range(8):
                def _w2ld(sy, c=c):
                    sy.dma_start(out=P1[:, 0:4096].rearrange("p (a c) -> p a c", c=1024),
                                 in_=w2T_d.rearrange("(a p) c -> p a c", p=128)[:, c * 4:(c + 1) * 4, :]).then_inc(SY, 16)
                ch.step("sync", _w2ld, dmas=1)

                def _w2q(v, c=c):
                    v.tensor_scalar(st8[:, 32:40], st16[:, 40:48], 1.0, None, ALU.mult)
                    v.tensor_mul(P1[:, 4096:6144], P1[:, 0:2048], st8[:, 39:40].broadcast_to((128, 2048)))
                    v.tensor_mul(P1[:, 6144:8192], P1[:, 2048:4096], st8[:, 39:40].broadcast_to((128, 2048)))
                    for hh in range(4):
                        rne(v, P1[:, hh * 1024:(hh + 1) * 1024], P1[:, 4096 + hh * 1024:4096 + (hh + 1) * 1024],
                            P2[:, 0:1024], P2[:, 1024:2048])
                    v.tensor_scalar(P1[:, 4096:8192], P1[:, 0:4096], 1.0, -1.0, ALU.min, ALU.max)
                    return v.tensor_copy(WB[:, 0:4096], P1[:, 4096:8192])
                ch.step("vector", _w2q)

                for tc_ in range(4):
                    def _hld(sy, c=c, tc_=tc_):
                        sy.dma_start(out=P1[:, 4096:6144].rearrange("p (a c) -> p a c", c=512),
                                     in_=h_park[c * 512:(c + 1) * 512, tc_ * 512:(tc_ + 1) * 512].rearrange(
                                         "(a p) t -> p a t", p=128)).then_inc(SY, 16)
                    ch.step("sync", _hld, dmas=1)

                    def _bcs(t, tc_=tc_):
                        return t.matmul(pC[:, 0:512], ones_row[:], rowA[0:1, tc_ * 512:(tc_ + 1) * 512],
                                        start=True, stop=True)
                    ch.step("tensor", _bcs)

                    def _qh(v, c=c, tc_=tc_):
                        v.tensor_mul(P1[:, 6144:8192].rearrange("p (a c) -> p a c", c=512),
                                     P1[:, 4096:6144].rearrange("p (a c) -> p a c", c=512),
                                     pC[:, 0:512].unsqueeze(1).broadcast_to((128, 4, 512)))
                        rne(v, P1[:, 0:1024], P1[:, 6144:7168], P2[:, 0:1024], P2[:, 1024:2048])
                        rne(v, P1[:, 1024:2048], P1[:, 7168:8192], P2[:, 0:1024], P2[:, 1024:2048])
                        return v.tensor_copy(WB[:, 4096:6144], P1[:, 0:2048])
                    ch.step("vector", _qh)

                    for half in range(2):
                        def _w2mmh(t, c=c, tc_=tc_, half=half):
                            for dmo in range(8):
                                for kd in range(4):
                                    ins = t.matmul(pS[:, dmo * 256:dmo * 256 + 256],
                                                   WB[:, kd * 1024 + dmo * 128:kd * 1024 + (dmo + 1) * 128],
                                                   WB[:, 4096 + kd * 512 + half * 256:4096 + kd * 512 + half * 256 + 256],
                                                   start=(kd == 0), stop=(kd == 3))
                            return ins
                        ch.step("tensor", _w2mmh)

                        def _fold(t, tc_=tc_, half=half):
                            return t.matmul(pC[:, 0:256], ones_row[:],
                                            rowC[0:1, tc_ * 512 + half * 256:tc_ * 512 + half * 256 + 256],
                                            start=True, stop=True)
                        ch.step("tensor", _fold)

                        ch.step("vector", lambda v: v.tensor_copy(P1[:, 2048:2304], pC[:, 0:256]))

                        def _fev(v, half=half):
                            return v.tensor_mul(P1[:, 0:2048].rearrange("p (a c) -> p a c", c=256),
                                                pS[:, 0:2048].rearrange("p (a c) -> p a c", c=256),
                                                P1[:, 2048:2304].unsqueeze(1).broadcast_to((128, 8, 256)))
                        ch.step("vector", _fev)

                        for qh_ in range(2):
                            def _w2tr(t, qh_=qh_):
                                for dmo in range(8):
                                    ins = t.transpose(pA[:, dmo * 128:(dmo + 1) * 128],
                                                      P1[:, dmo * 256 + qh_ * 128:dmo * 256 + (qh_ + 1) * 128],
                                                      ident[:])
                                return ins
                            ch.step("tensor", _w2tr)

                            def _w2ad(v, tc_=tc_, half=half, qh_=qh_):
                                qg = tc_ * 4 + half * 2 + qh_
                                return resadd(v, qg, pA[:, 0:1024], 1024, P1[:, 8192:9216])
                            ch.step("vector", _w2ad)

        def emit_ffn_l1():
            for ic in range(NIC):
                icw = IC_W[ic]
                ico = IC_OFF[ic]
                na = icw // 128

                def _wgl(sy, ico=ico, icw=icw, na=na):
                    sy.dma_start(out=P1[:, 0:na * 1024].rearrange("p (a c) -> p a c", c=1024),
                                 in_=wgate_d.rearrange("(a p) c -> p a c", p=128)[:, ico // 128:ico // 128 + na, :]).then_inc(SY, 16)
                    sy.dma_start(out=P1[:, 4096:4096 + na * 1024].rearrange("p (a c) -> p a c", c=1024),
                                 in_=wup_d.rearrange("(a p) c -> p a c", p=128)[:, ico // 128:ico // 128 + na, :]).then_inc(SY, 16)
                    sy.dma_start(out=P2[:, 0:8 * icw].rearrange("p (a c) -> p a c", c=icw) if icw <= 512 else None,
                                 in_=wdown_d.rearrange("(a p) c -> p a c", p=128)[:, :, ico:ico + icw]).then_inc(SY, 16)
                ch.step("sync", _wgl, dmas=3)

                tot_g = na * 1024
                for (off, tot) in ((0, tot_g), (4096, tot_g)):
                    for hh in range((tot + 2047) // 2048):
                        w0 = off + hh * 2048
                        wd_ = min(off + tot, w0 + 2048) - w0
                        nb = wd_ // 16

                        def _fq(v, w0=w0, wd_=wd_, nb=nb):
                            fp4q(v, WP[:, 8192:8192 + wd_], P1[:, w0:w0 + wd_],
                                 WP[:, 0:wd_], WP[:, 2048:2048 + wd_], WP[:, 4096:4096 + wd_],
                                 st16[:, 0:nb], st16[:, 128:128 + nb], nblk=nb)
                            return v.tensor_copy(P1[:, w0:w0 + wd_], WP[:, 8192:8192 + wd_])
                        ch.step("vector", _fq)

                for mi, off in enumerate((0, 4096)):
                    for a in range(na):
                        def _wtr(t, a=a, off=off):
                            for dd in range(8):
                                ins = t.transpose(pS[:, dd * 128:(dd + 1) * 128],
                                                  P1[:, off + a * 1024 + dd * 128:off + a * 1024 + (dd + 1) * 128],
                                                  ident[:])
                            return ins
                        ch.step("tensor", _wtr)

                        def _wte(v, a=a, mi=mi, icw=icw):
                            return v.tensor_copy(
                                WP[:, mi * 4096:mi * 4096 + 8 * icw].rearrange("p (d c) -> p d c", c=icw)[:, :, a * 128:(a + 1) * 128],
                                pS[:, 0:1024].rearrange("p (d c) -> p d c", c=128))
                        ch.step("vector", _wte)
                tot_d = 8 * icw
                for hh in range((tot_d + 2047) // 2048):
                    w0 = hh * 2048
                    wd_ = min(tot_d, w0 + 2048) - w0
                    nb = wd_ // 16

                    def _fqd(v, w0=w0, wd_=wd_, nb=nb):
                        fp4q(v, P1[:, 0:wd_], P2[:, w0:w0 + wd_],
                             P1[:, 2048:2048 + wd_], P1[:, 4096:4096 + wd_], P1[:, 6144:6144 + wd_],
                             st16[:, 0:nb], st16[:, 128:128 + nb], nblk=nb)
                        return v.tensor_copy(P2[:, w0:w0 + wd_], P1[:, 0:wd_])
                    ch.step("vector", _fqd)

                for dd in range(8):
                    def _dtr(t, dd=dd, icw=icw, na=na):
                        for a in range(na):
                            ins = t.transpose(pS[:, a * 128:(a + 1) * 128],
                                              P2[:, dd * icw + a * 128:dd * icw + (a + 1) * 128],
                                              ident[:])
                        return ins
                    ch.step("tensor", _dtr)

                    def _dte(v, dd=dd, na=na):
                        return v.tensor_copy(
                            WP[:, 8192:8192 + na * 1024].rearrange("p (a c) -> p a c", c=1024)[:, :, dd * 128:(dd + 1) * 128],
                            pS[:, 0:na * 128].rearrange("p (a c) -> p a c", c=128))
                    ch.step("vector", _dte)

                for ti in range(NT):
                    def _xql(sy, ti=ti):
                        sy.dma_start(out=P1[:, 0:1024].rearrange("p (a c) -> p a c", c=128),
                                     in_=xqT_park.rearrange("(a p) t -> p a t", p=128)[:, :, ti * 128:(ti + 1) * 128]).then_inc(SY, 16)
                    ch.step("sync", _xql, dmas=1)

                    def _gmm(t, icw=icw):
                        for kd in range(8):
                            ins = t.matmul(pB[:, 0:icw], P1[:, kd * 128:(kd + 1) * 128],
                                           WP[:, kd * icw:(kd + 1) * icw],
                                           start=(kd == 0), stop=(kd == 7))
                        return ins
                    ch.step("tensor", _gmm)

                    def _umm(t, icw=icw):
                        for kd in range(8):
                            ins = t.matmul(pC[:, 0:icw], P1[:, kd * 128:(kd + 1) * 128],
                                           WP[:, 4096 + kd * icw:4096 + (kd + 1) * icw],
                                           start=(kd == 0), stop=(kd == 7))
                        return ins
                    ch.step("tensor", _umm)

                    ch.step("scalar", lambda s, icw=icw: s.activation(P1[:, 1024:1024 + icw], pB[:, 0:icw], ACT.Silu))

                    def _h1(v, icw=icw):
                        return v.tensor_mul(P1[:, 2048:2048 + icw], P1[:, 1024:1024 + icw], pC[:, 0:icw])
                    ch.step("vector", _h1)

                    def _h1q(v, icw=icw):
                        nb = icw // 16
                        return fp4q(v, P1[:, 6144:6144 + icw], P1[:, 2048:2048 + icw],
                                    P1[:, 3072:3072 + icw], P1[:, 4096:4096 + icw], P1[:, 5120:5120 + icw],
                                    st16[:, 0:nb], st16[:, 128:128 + nb], nblk=nb)
                    ch.step("vector", _h1q)

                    def _htr(t, icw=icw, na=na):
                        for a in range(na):
                            ins = t.transpose(pA[:, a * 128:(a + 1) * 128],
                                              P1[:, 6144 + a * 128:6144 + (a + 1) * 128], ident[:])
                        return ins
                    ch.step("tensor", _htr)

                    def _hte(v, na=na):
                        return v.tensor_copy(P1[:, 7168:7168 + na * 128], pA[:, 0:na * 128])
                    ch.step("vector", _hte)

                    def _dmm(t, na=na):
                        for a in range(na):
                            for n0 in (0, 512):
                                ins = t.matmul(pS[:, n0:n0 + 512], P1[:, 7168 + a * 128:7168 + (a + 1) * 128],
                                               WP[:, 8192 + a * 1024 + n0:8192 + a * 1024 + n0 + 512],
                                               start=(a == 0), stop=(a == na - 1))
                        return ins
                    ch.step("tensor", _dmm)

                    def _dadd(v, ti=ti):
                        return resadd(v, ti, pS[:, 0:1024], 1024, P1[:, 8192:9216])
                    ch.step("vector", _dadd)

        if upto >= 2:
            emit_ln_prep(0)
            emit_l0_actscales()
        if upto >= 2.1:
            emit_ternary_wqkv()
        if upto >= 2.2:
            emit_qkv_l0()
        if upto >= 2.3:
            emit_ternary_wproj()
        if upto >= 2.4:
            emit_attn(0)
        if upto >= 3:
            emit_ffn_l0()
        if upto >= 4:
            emit_ln_prep(1)
            emit_qkv_l1()

            def _wol(sy):
                sy.dma_start(out=WP[:, 0:8192].rearrange("p (a c) -> p a c", c=1024),
                             in_=wo1T_d.rearrange("(a p) c -> p a c", p=128)).then_inc(SY, 16)
            ch.step("sync", _wol, dmas=1)
            emit_attn(1)
        if upto >= 5:
            emit_ffn_l1()

        twp = tap("wps", [128, 12288])
        if twp is not None:
            def _twp(sy):
                sy.dma_start(out=twp, in_=WP[:]).then_inc(SY, 16)
            ch.step("sync", _twp, dmas=1)
        tqx = tap("qxs", [D, T], BF16)
        if tqx is not None:
            def _tqx(sy):
                sy.dma_start(out=tqx, in_=qxT_park).then_inc(SY, 16)
            ch.step("sync", _tqx, dmas=1)
        tra = tap("rowAs", [1, T])
        if tra is not None:
            def _tra(sy):
                sy.dma_start(out=tra, in_=rowA[:]).then_inc(SY, 16)
            ch.step("sync", _tra, dmas=1)
        trb = tap("rowBs", [1, 256])
        if trb is not None:
            def _trb(sy):
                sy.dma_start(out=trb, in_=rowB[:]).then_inc(SY, 16)
            ch.step("sync", _trb, dmas=1)
        thp = tap("hps", [FI, T])
        if thp is not None:
            def _thp(sy):
                sy.dma_start(out=thp, in_=h_park).then_inc(SY, 16)
            ch.step("sync", _thp, dmas=1)
        t16 = tap("st16s", [128, 256])
        if t16 is not None:
            def _t16(sy):
                sy.dma_start(out=t16, in_=st16[:]).then_inc(SY, 16)
            ch.step("sync", _t16, dmas=1)
        t15 = tap("xnT1s", [128, D])
        if t15 is not None:
            def _t15(sy):
                sy.dma_start(out=t15, in_=P1[:, 5120:6144]).then_inc(SY, 16)
            ch.step("sync", _t15, dmas=1)
        t15b = tap("xn1s", [128, D])
        if t15b is not None:
            def _t15b(sy):
                sy.dma_start(out=t15b, in_=P1[:, 0:1024]).then_inc(SY, 16)
            ch.step("sync", _t15b, dmas=1)
        for pname, park, rows in (("xnpark", xnT_park, D), ("qpark", qT_park, D),
                                  ("kpark", kT_park, D), ("vpark", v_park, T)):
            pt = tap(pname, [rows, 1040 if pname == "vpark" else T])
            if pt is not None:
                def _pt(sy, pt=pt, park=park):
                    sy.dma_start(out=pt, in_=park).then_inc(SY, 16)
                ch.step("sync", _pt, dmas=1)

        xr_tap = tap("x_res", [128, NT * D])
        if xr_tap is not None:
            def _xrt(sy):
                sy.dma_start(out=xr_tap, in_=x_res[:]).then_inc(SY, 16)
            ch.step("sync", _xrt, dmas=1)

        def _store(sy):
            sy.dma_start(out=y_out.rearrange("(t p) d -> p t d", p=128),
                         in_=x_res[:].rearrange("p (t d) -> p t d", d=D)).then_inc(SY, 16)
        ch.step("sync", _store, dmas=1)

        ch.emit(fence_proxy=FenceProxy)
        with nc.Block() as blk:
            @blk.sync
            def _(sy):
                sy.wait_ge(SY, ch.cnt["sync"])
    return nc, dbg


def host_tables():
    import jax
    cpu = jax.devices("cpu")[0]
    import jax.numpy as jnp
    with jax.default_device(cpu):
        inv_freq = 1.0 / jnp.asarray(10000.0, jnp.float32) ** (
            jnp.arange(0, 64, 2, dtype=jnp.float32) / 64)
        freqs = jnp.outer(jnp.arange(T, dtype=jnp.float32), inv_freq)
        cos0 = np.asarray(jnp.cos(freqs), np.float32)
        sin0 = np.asarray(jnp.sin(freqs), np.float32)
        ang1 = np.asarray(jnp.power(jnp.asarray(10000.0, jnp.float32),
                                    2.0 * jnp.arange(32, dtype=jnp.float32) / 64), np.float32)
    return cos0, sin0, np.ascontiguousarray(ang1.reshape(1, 32))


def kernel(L0_attn_x, L0_attn_w_qkv, L0_attn_w_proj, L0_attn_q_gain,
           L0_ffn_w1, L0_ffn_conv_w, L0_ffn_conv_b, L0_ffn_w2,
           L1_attn_w_q, L1_attn_w_k, L1_attn_w_v, L1_attn_w_o,
           L1_ffn_w_gate, L1_ffn_w_up, L1_ffn_w_down, _debug_taps=(), _upto=99):
    cos0, sin0, ang1 = host_tables()
    asc = np.ascontiguousarray
    base = {
        "w_qkv": asc(np.asarray(L0_attn_w_qkv, np.float32)),
        "w_proj": asc(np.asarray(L0_attn_w_proj, np.float32)),
        "q_gain": asc(np.asarray(L0_attn_q_gain, np.float32).reshape(1, 16)),
        "w1T": asc(np.asarray(L0_ffn_w1, np.float32).T),
        "conv_w": asc(np.asarray(L0_ffn_conv_w, np.float32).reshape(FI, 3)),
        "conv_b": asc(np.asarray(L0_ffn_conv_b, np.float32).reshape(FI, 1)),
        "w2T": asc(np.asarray(L0_ffn_w2, np.float32).T),
        "wq1T": asc(np.asarray(L1_attn_w_q, np.float32).T),
        "wk1T": asc(np.asarray(L1_attn_w_k, np.float32).T),
        "wv1T": asc(np.asarray(L1_attn_w_v, np.float32).T),
        "wo1T": asc(np.asarray(L1_attn_w_o, np.float32).T),
        "w_gate": asc(np.asarray(L1_ffn_w_gate, np.float32)),
        "w_up": asc(np.asarray(L1_ffn_w_up, np.float32)),
        "w_down": asc(np.asarray(L1_ffn_w_down, np.float32)),
        "cos0": cos0, "sin0": sin0, "ang1": ang1,
    }
    x = np.asarray(L0_attn_x, np.float32)
    nc, dbg = build_nc(_debug_taps, upto=_upto)
    in_maps = [dict(base, x=asc(x[b])) for b in range(8)]
    res = run_bass_kernel_spmd(nc, in_maps, core_ids=list(range(8)))
    out = np.stack([np.asarray(res.results[b]["y"]) for b in range(8)], axis=0)
    if _debug_taps:
        return out, [{k: np.asarray(r["dbg_" + k]) for k in _debug_taps} for r in res.results]
    return out
